# version 15
# baseline (speedup 1.0000x reference)
import numpy as np
import ml_dtypes
import concourse.bass as bass
import concourse.mybir as mybir
from concourse.bass_utils import run_bass_kernel_spmd

F32 = mybir.dt.float32
BF16 = mybir.dt.bfloat16
AF = mybir.ActivationFunctionType
ALU = mybir.AluOpType
AX = mybir.AxisListType
bfdt = ml_dtypes.bfloat16

D = 1024
H = 16
DK = 64
E = 16
NI = 1024
GU = 2 * NI
C = 128
KC = D // 128
LN_EPS = 1e-5
ALPHA = 1.702
LIMIT = 7.0
NEG_BIG = -1.0e9


class Sched:

    def __init__(self, nc):
        self.nc = nc
        self.ops = []
        self.cnt = {}
        self.sems = {}

    def sem(self, name):
        self.sems[name] = self.nc.alloc_semaphore(name=name)
        self.cnt[name] = 0
        return name

    def op(self, eng, fn, waits=(), inc=None, n=1, fence=True):
        if inc in ("SD", "PD"):
            n = 16
        waits = tuple((s, v) for (s, v) in waits if v > 0)
        for s, v in waits:
            assert v <= self.cnt[s], f"future wait {s} {v} > {self.cnt[s]}"
        if inc is not None:
            self.cnt[inc] += n
        self.ops.append((eng, waits, fn, inc, n, fence))

    def val(self, sem):
        return self.cnt[sem]

    def emit_all(self, block):
        engmap = {
            "tensor": block.tensor, "vector": block.vector,
            "scalar": block.scalar, "sync": block.sync, "gpsimd": block.gpsimd,
        }
        selfsem = {"vector": "DV", "scalar": "AC", "gpsimd": "PL"}
        for name, deco in engmap.items():
            ops = [o for o in self.ops if o[0] == name]

            def make(ops_, name_):
                def body(eng):
                    seen = {}
                    own = selfsem.get(name_)
                    cnt = 0
                    for (_e, waits, fn, inc, n, fence) in ops_:
                        if own is not None and cnt > 0 and fence:
                            if seen.get(own, 0) < cnt:
                                eng.wait_ge(self.sems[own], cnt)
                                seen[own] = cnt
                        for (s, v) in waits:
                            if seen.get(s, 0) < v:
                                eng.wait_ge(self.sems[s], v)
                                seen[s] = v
                        inst = fn(eng)
                        if inc is not None:
                            inst.then_inc(self.sems[inc], n)
                            if inc == own:
                                cnt += n
                return body
            deco(make(ops, name))


def build_nc(NSEQ=2, L=4096, G=512, n_dbg=0, phase_limit=99, dump=False):
    nc = bass.Bass("TRN2", target_bir_lowering=False,
                   detect_race_conditions=False)
    plmaj = 4 if 40 <= phase_limit < 90 else phase_limit
    NCH = (L // G) * NSEQ
    TT = G // 128
    SUB = G // C

    def din(name, shape, dt=F32):
        return nc.dram_tensor(name, list(shape), dt, kind="ExternalInput").ap()

    x_t = din("x_t", [NSEQ, D, L])
    wqkv_t = din("wqkv_t", [D, 3 * D])
    wo_t = din("wo_t", [D, D])
    router_t = din("router_t", [D, E])
    log_gamma = din("log_gamma", [H])
    r_bias = din("r_bias", [E])
    gub_t = din("gub_t", [128, 256])
    gu_proj = din("gu_proj", [E, D, GU])
    down_proj = din("down_proj", [E, NI, D])
    down_bias = din("down_bias", [E, D])
    c_ident = din("c_ident", [128, 128], BF16)
    c_idxT = din("c_idxT", [128, 128])
    c_maskT = din("c_maskT", [128, 128])
    c_e1 = din("c_e1", [128, 128])
    c_c2 = din("c_c2", [128, 1])
    c_ones1 = din("c_ones1", [1, 128], BF16)
    c_onescol = din("c_onescol", [128, 1], BF16)

    out = nc.dram_tensor("out", [NSEQ, L, D], F32, kind="ExternalOutput").ap()
    dbg = None
    if n_dbg:
        dbg = nc.dram_tensor("dbg", [n_dbg, 128, 4096], F32,
                             kind="ExternalOutput").ap()

    wqkv_cache = nc.dram_tensor("wqkv_cache", [D, 3 * D], BF16, kind="Internal").ap()
    wo_cache = nc.dram_tensor("wo_cache", [D, D], BF16, kind="Internal").ap()
    gu_cache = nc.dram_tensor("gu_cache", [E, D, GU], BF16, kind="Internal").ap()
    down_cache = nc.dram_tensor("down_cache", [E, NI, D], BF16, kind="Internal").ap()

    A = nc.alloc_sbuf_tensor
    s_big = A("s_big", [128, KC * 3 * D], BF16)
    wq_sl = lambda kc, m0, w: s_big[:, kc * 3 * D + m0: kc * 3 * D + m0 + w]
    gu_sl = lambda kc, n0, w: s_big[:, kc * GU + n0: kc * GU + n0 + w]
    s_r2 = A("s_r2", [128, KC * D], BF16)
    wo_sl = lambda kc, m0, w: s_r2[:, kc * D + m0: kc * D + m0 + w]
    dn_sl = lambda kc, h0, w: s_r2[:, kc * D + h0: kc * D + h0 + w]

    s_xw = A("s_xw", [128, KC * G], BF16)
    s_qT = A("s_qT", [128, KC * G], BF16)
    s_qR = A("s_qR", [128, KC * G], BF16)
    s_yT = A("s_yT", [128, KC * G], BF16)
    s_xres = A("s_xres", [128, KC * G], BF16)
    x1tok_sl = lambda tt: s_xres[:, tt * D:(tt + 1) * D]
    s_x1 = A("s_x1", [128, KC * G], BF16)
    s_act = A("s_act", [128, KC * G], BF16)
    s_kn = A("s_kn", [128, 2 * D], BF16)
    s_knT = A("s_knT", [128, 2 * KC * 128], BF16)
    s_v = A("s_v", [128, 2 * D], BF16)
    s_kp = A("s_kp", [128, D], BF16)
    s_sdec = A("s_sdec", [128, 2 * 256], BF16)
    s_acc = A("s_acc", [128, TT * D], F32)
    s_stage = A("s_stage", [128, 2 * D], F32)
    s_r = A("s_r", [128, 2 * D], F32)
    s_scr = A("s_scr", [128, 4 * 512], F32)
    s_eps = A("s_eps", [128, 1], F32)
    scr = lambda i, w=512: s_scr[:, i * 512: i * 512 + w]
    s_sf = A("s_sf", [128, 2 * 8 * DK], F32)
    s_sbf = A("s_sbf", [128, 2 * 8 * DK], BF16)
    sf_sl = lambda par, hp: s_sf[:, par * 512 + hp * DK: par * 512 + (hp + 1) * DK]
    sbf_sl = lambda par, hp: s_sbf[:, par * 512 + hp * DK: par * 512 + (hp + 1) * DK]

    s_ident = A("s_ident", [128, 128], BF16)
    s_idxT = A("s_idxT", [128, 128], F32)
    s_maskT = A("s_maskT", [128, 128], F32)
    s_e1 = A("s_e1", [128, 128], F32)
    s_c2 = A("s_c2", [128, 1], F32)
    s_ones1 = A("s_ones1", [1, 128], BF16)
    s_onescol = A("s_onescol", [128, 1], BF16)
    s_dmat = A("s_dmat", [128, 8 * 256], BF16)
    s_gpow = A("s_gpow", [128, 8 * 128], BF16)
    s_lgbc = A("s_lgbc", [128, H], F32)
    s_brep = A("s_brep", [128, E], F32)
    s_llg = A("s_llg", [128, H], F32)
    s_gC = A("s_gC", [128, H], F32)
    s_gCp = A("s_gCp", [128, 8], F32)
    s_llgp = A("s_llgp", [128, 8], F32)
    s_kpow = A("s_kpow", [128, H], F32)
    s_gub = A("s_gub", [128, 256], F32)
    s_gubsc = A("s_gubsc", [128, 128], F32)
    s_gubu1 = A("s_gubu1", [128, 128], F32)
    s_rt = A("s_rt", [128, KC * E], BF16)
    s_db = A("s_db", [16, D], BF16)
    s_lg = A("s_lg", [128, 16 * 8], F32)
    s_w = A("s_w", [128, TT * E], F32)
    s_wbf = A("s_wbf", [128, E], BF16)
    s_wT = A("s_wT", [16, TT * 128], BF16)
    s_top8 = A("s_top8", [128, 8], F32)
    s_sm = A("s_sm", [128, 64], F32)
    sm = lambda i, w=1: s_sm[:, i: i + w]
    s_ln = A("s_ln", [1, 3 * 512], F32)
    lnr = lambda i: s_ln[0:1, i * 512: (i + 1) * 512]
    s_lnb = A("s_lnb", [1, 2 * 512], BF16)
    lnb = lambda i: s_lnb[0:1, i * 512: (i + 1) * 512]
    s_k16 = A("s_k16", [128, 6 * 16], F32)
    k16 = lambda i: s_k16[:, i * 16: (i + 1) * 16]

    P = nc.alloc_psum_tensor
    pA = P("pA", [128, 512], F32)
    pB = P("pB", [128, 512], F32)
    pC = P("pC", [128, 512], F32)
    pD = P("pD", [128, 512], F32)
    pE = P("pE", [128, 512], F32)
    pF = P("pF", [128, 512], F32)
    pG = P("pG", [128, 1024], BF16)

    S = Sched(nc)
    PE = S.sem("PE"); DV = S.sem("DV"); AC = S.sem("AC")
    SD = S.sem("SD"); PD = S.sem("PD"); PL = S.sem("PL")

    bank_free = {}
    bank_full = {}

    def bname(bank):
        return id(bank)

    def mm_group(bank, mms, extra_waits=()):
        name = bname(bank)
        w = list(extra_waits)
        if name in bank_free:
            w.append(bank_free[name])
        for i, fn in enumerate(mms):
            S.op("tensor", fn, waits=tuple(w) if i == 0 else (),
                 inc=PE if i == len(mms) - 1 else None)
        bank_full[name] = (PE, S.val(PE))

    def bank_read(bank, eng, fn, extra_waits=(), inc=None, fence=True):
        name = bname(bank)
        w = list(extra_waits)
        if name in bank_full:
            w.append(bank_full[name])
        semmap = {"vector": DV, "scalar": AC, "gpsimd": PL}
        s = semmap[eng] if inc is None else inc
        S.op(eng, fn, waits=tuple(w), inc=s, fence=fence)
        bank_free[name] = (s, S.val(s))

    last_read = {}

    def note_read(key, sem):
        last_read[key] = (sem, S.val(sem))

    def rd(key):
        return [last_read[key]] if key in last_read else []

    for dst, src in [(s_ident, c_ident), (s_idxT, c_idxT), (s_maskT, c_maskT),
                     (s_e1, c_e1), (s_c2, c_c2), (s_ones1, c_ones1),
                     (s_onescol, c_onescol), (s_gub, gub_t)]:
        S.op("sync", lambda e, d=dst, s=src: e.dma_start(out=d[:], in_=s),
             inc=SD)
    S.op("sync", lambda e: e.dma_start(
        out=s_lgbc[:], in_=log_gamma.unsqueeze(0).to_broadcast([128, H])), inc=SD)
    S.op("sync", lambda e: e.dma_start(
        out=s_brep[:], in_=r_bias.unsqueeze(0).to_broadcast([128, E])), inc=SD)
    consts_ld = (SD, S.val(SD))
    S.op("gpsimd", lambda e: e.dma_start(
        out=s_rt[:].rearrange("p (kc e) -> p kc e", kc=KC),
        in_=router_t.rearrange("(kc p) e -> p kc e", p=128)), inc=PD)
    S.op("gpsimd", lambda e: e.dma_start(out=s_db[:], in_=down_bias), inc=PD)
    pre_pd = (PD, S.val(PD))

    S.op("scalar", lambda e: nc.scalar.activation(
        s_llg[:], s_lgbc[:], AF.Sigmoid), waits=[consts_ld], inc=AC)
    S.op("scalar", lambda e: nc.scalar.activation(
        s_llg[:], s_llg[:], AF.Ln), inc=AC)
    S.op("scalar", lambda e: nc.scalar.activation(
        s_gC[:], s_llg[:], AF.Exp, scale=float(C)), inc=AC)
    llg_done = (AC, S.val(AC))
    S.op("vector", lambda e: e.tensor_scalar(
        s_kpow[:], s_llg[:], s_c2[:], None, op0=ALU.mult),
        waits=[llg_done], inc=DV)
    S.op("scalar", lambda e: nc.scalar.activation(
        s_kpow[:], s_kpow[:], AF.Exp), waits=[(DV, S.val(DV))], inc=AC)
    S.op("vector", lambda e: e.tensor_copy(
        s_llgp[0:64, :], s_llg[0:64, :].rearrange("p (a two) -> p a two", two=2)[:, :, 0]),
        waits=[llg_done], inc=DV)
    S.op("vector", lambda e: e.tensor_copy(
        s_llgp[64:128, :], s_llg[64:128, :].rearrange("p (a two) -> p a two", two=2)[:, :, 1]),
        inc=DV)
    S.op("vector", lambda e: e.tensor_copy(
        s_gCp[0:64, :], s_gC[0:64, :].rearrange("p (a two) -> p a two", two=2)[:, :, 0]),
        inc=DV)
    S.op("vector", lambda e: e.tensor_copy(
        s_gCp[64:128, :], s_gC[64:128, :].rearrange("p (a two) -> p a two", two=2)[:, :, 1]),
        inc=DV)
    pair_done = (DV, S.val(DV))
    for c in range(8):
        S.op("scalar", lambda e, c=c: nc.scalar.activation(
            s_gpow[:, c * 128:(c + 1) * 128], s_e1[:], AF.Exp,
            scale=s_llgp[:, c: c + 1]), waits=[pair_done], inc=AC)
    for h in range(H):
        hp, lo = divmod(h, 2)
        col = hp * 256 + lo * 128
        S.op("vector", lambda e, h=h: e.scalar_tensor_tensor(
            scr(0, 128), s_idxT[:], s_llg[:, h: h + 1], s_maskT[:],
            op0=ALU.mult, op1=ALU.add), waits=[llg_done, consts_ld], inc=DV)
        S.op("scalar", lambda e, col=col: nc.scalar.activation(
            s_dmat[:, col: col + 128], scr(0, 128), AF.Exp),
            waits=[(DV, S.val(DV))], inc=AC)
        S.op("vector", lambda e: e.engine_nop(), waits=[(AC, S.val(AC))], inc=DV)
    S.op("vector", lambda e: e.tensor_scalar(
        s_gubsc[:].rearrange("p (e j) -> p e j", e=E),
        s_gub[:].rearrange("p (e n) -> p e n", e=E)[:, :, 0:8],
        ALPHA, None, op0=ALU.mult), waits=[consts_ld], inc=DV)
    S.op("vector", lambda e: e.tensor_scalar(
        s_gubu1[:].rearrange("p (e j) -> p e j", e=E),
        s_gub[:].rearrange("p (e n) -> p e n", e=E)[:, :, 8:16],
        1.0, None, op0=ALU.add), waits=[consts_ld], inc=DV)
    S.op("vector", lambda e: e.memset(s_sf[:], 0.0), inc=DV)
    S.op("vector", lambda e: e.memset(s_eps[:], LN_EPS), inc=DV)
    S.op("vector", lambda e: e.memset(s_sbf[:], 0.0), inc=DV)
    prelude_dv = (DV, S.val(DV))
    prelude_ac = (AC, S.val(AC))


    C7 = LIMIT
    SV_CLIP = float(ALPHA * LIMIT)

    dbg_next = [0]

    def dbg_dump(src_ap, waits):
        if dbg is None or dbg_next[0] >= n_dbg:
            return
        i = dbg_next[0]
        dbg_next[0] += 1
        S.op("sync", lambda e: e.dma_start(out=dbg[i], in_=src_ap),
             waits=waits, inc=SD)

    for ch in range(NCH):
        b, g = divmod(ch, L // G)
        t0 = g * G
        first = (ch == 0)
        seq_start = (g == 0)

        S.op("gpsimd", lambda e, b=b, t0=t0: e.dma_start(
            out=s_xw[:].rearrange("p (kc t) -> p kc t", kc=KC),
            in_=x_t[b, :, t0:t0 + G].rearrange("(kc p) t -> p kc t", p=128)),
            waits=rd("xw") + ([pre_pd] if first else []), inc=PD)
        xw_ld = (PD, S.val(PD))

        if first:
            S.op("gpsimd", lambda e: e.dma_start(
                out=s_big[:].rearrange("p (kc m) -> p kc m", kc=KC),
                in_=wqkv_t.rearrange("(kc p) m -> p kc m", p=128)), inc=PD)
            wq_ld = (PD, S.val(PD))
            S.op("gpsimd", lambda e: e.dma_start(
                out=wqkv_cache.rearrange("(kc p) m -> p kc m", p=128),
                in_=s_big[:].rearrange("p (kc m) -> p kc m", kc=KC)),
                waits=[wq_ld], inc=PD)
            wq_wb = (PD, S.val(PD))
            S.op("gpsimd", lambda e: e.dma_start(
                out=s_r2[:].rearrange("p (kc m) -> p kc m", kc=KC),
                in_=wo_t.rearrange("(kc p) m -> p kc m", p=128)), inc=PD)
            wo_ld = (PD, S.val(PD))
            S.op("gpsimd", lambda e: e.dma_start(
                out=wo_cache.rearrange("(kc p) m -> p kc m", p=128),
                in_=s_r2[:].rearrange("p (kc m) -> p kc m", kc=KC)),
                waits=[wo_ld], inc=PD)
            wo_wb = (PD, S.val(PD))
        else:
            S.op("gpsimd", lambda e: e.dma_start(
                out=s_big[:].rearrange("p (kc m) -> p kc m", kc=KC),
                in_=wqkv_cache.rearrange("(kc p) m -> p kc m", p=128)),
                waits=rd("big") + [wq_wb], inc=PD)
            wq_ld = (PD, S.val(PD))
            S.op("gpsimd", lambda e: e.dma_start(
                out=s_r2[:].rearrange("p (kc m) -> p kc m", kc=KC),
                in_=wo_cache.rearrange("(kc p) m -> p kc m", p=128)),
                waits=rd("r2") + [wo_wb], inc=PD)
            wo_ld = (PD, S.val(PD))

        if seq_start and not first:
            S.op("vector", lambda e: e.memset(s_sf[:], 0.0),
                 waits=rd("state"), inc=DV)
            S.op("vector", lambda e: e.memset(s_sbf[:], 0.0), inc=DV)
        state_zeroed = (DV, S.val(DV)) if seq_start else None

        qT_ready = None
        if plmaj < 2:
            continue
        def emit_qpass():
            nonlocal qT_ready
            for mc in range(KC):
                bank = pA if mc % 2 == 0 else pB
                mms = []
                for kc in range(KC):
                    mms.append(lambda e, kc=kc, mc=mc, bank=bank:
                               nc.tensor.matmul(bank[:], wq_sl(kc, mc * 128, 128),
                                                s_xw[:, kc * G:(kc + 1) * G],
                                                start=(kc == 0), stop=(kc == KC - 1)))
                mm_group(bank, mms, extra_waits=[wq_ld, xw_ld])
                S.op("vector", lambda e, mc=mc, bank=bank: e.tensor_copy(
                    s_qR[:, mc * G:(mc + 1) * G], bank[:]),
                    waits=[bank_full[bname(bank)]] + (rd("qR") if mc == 0 else []),
                    inc=DV, fence=False)
                S_gp = s_gpow[:, mc * 128:(mc + 1) * 128]
                bank_read(bank, "vector",
                          lambda e, mc=mc, bank=bank, S_gp=S_gp: e.tensor_tensor(
                              s_qT[:, mc * G:(mc + 1) * G].rearrange(
                                  "p (tt c) -> p tt c", c=128),
                              bank[:].rearrange("p (tt c) -> p tt c", c=128),
                              S_gp.unsqueeze(1).to_broadcast([128, TT, 128]),
                              op=ALU.mult),
                          extra_waits=[prelude_ac] + rd("qT"), fence=False)
            qT_ready = (DV, S.val(DV))

        if plmaj < 3:
            continue
        kn_ready_m = {}
        v_ready_m = {}
        knT_ready_m = {}

        def emit_kv(tt):
            rot = tt % 2
            for mg in range(4):
                bank = pC if mg % 2 == 0 else pD
                m0 = D + mg * 512
                mms = []
                for kc in range(KC):
                    mms.append(lambda e, kc=kc, m0=m0, tt=tt, bank=bank:
                               nc.tensor.matmul(
                                   bank[:],
                                   s_xw[:, kc * G + tt * 128: kc * G + (tt + 1) * 128],
                                   wq_sl(kc, m0, 512),
                                   start=(kc == 0), stop=(kc == KC - 1)))
                mm_group(bank, mms, extra_waits=[wq_ld, xw_ld])
                if mg < 2:
                    kraw = scr(mg)
                    bank_read(bank, "vector",
                              lambda e, bank=bank, kraw=kraw: e.tensor_copy(
                                  kraw, bank[:]), fence=False)
                    S.op("scalar", lambda e, kraw=kraw: nc.scalar.activation(
                        scr(2), kraw, AF.Square), waits=[(DV, S.val(DV))], inc=AC,
                        fence=False)
                    S.op("vector", lambda e, kraw=kraw, mg=mg: e.tensor_reduce(
                        k16(0)[:, mg * 8:(mg + 1) * 8],
                        kraw.rearrange("p (h d) -> p h d", d=DK),
                        axis=AX.X, op=ALU.add), inc=DV)
                    S.op("vector", lambda e, mg=mg: e.tensor_reduce(
                        k16(1)[:, mg * 8:(mg + 1) * 8],
                        scr(2).rearrange("p (h d) -> p h d", d=DK),
                        axis=AX.X, op=ALU.add), waits=[(AC, S.val(AC))], inc=DV,
                        fence=False)
                else:
                    if mg == 2:
                        emit_stats(tt, 0)
                    v0 = (mg - 2) * 512
                    bank_read(bank, "vector",
                              lambda e, bank=bank, v0=v0, rot=rot: e.tensor_copy(
                                  s_v[:, rot * D + v0: rot * D + v0 + 512],
                                  bank[:]), extra_waits=rd(("v", rot)),
                              fence=False)
            emit_stats(tt, 1)
            v_ready_m[tt] = (DV, S.val(DV))

        def emit_stats(tt, mg):
            rot = tt % 2
            hs = slice(mg * 8, (mg + 1) * 8)
            S.op("vector", lambda e, hs=hs: e.tensor_scalar(
                k16(2)[:, hs], k16(0)[:, hs], 1.0 / DK, None, op0=ALU.mult),
                inc=DV)
            S.op("vector", lambda e, hs=hs: e.tensor_scalar(
                k16(3)[:, hs], k16(1)[:, hs], 1.0 / DK, None, op0=ALU.mult),
                inc=DV)
            S.op("vector", lambda e, hs=hs: e.tensor_tensor(
                k16(4)[:, hs], k16(2)[:, hs], k16(2)[:, hs], op=ALU.mult),
                inc=DV)
            S.op("vector", lambda e, hs=hs: e.tensor_tensor(
                k16(3)[:, hs], k16(3)[:, hs], k16(4)[:, hs], op=ALU.subtract),
                inc=DV)
            S.op("scalar", lambda e, hs=hs: nc.scalar.activation(
                k16(3)[:, hs], k16(3)[:, hs], AF.Ln, bias=s_eps[:]),
                waits=[(DV, S.val(DV)), prelude_dv], inc=AC)
            S.op("scalar", lambda e, hs=hs: nc.scalar.activation(
                k16(3)[:, hs], k16(3)[:, hs], AF.Exp, scale=-0.5), inc=AC)
            S.op("vector", lambda e, hs=hs: e.tensor_tensor(
                k16(4)[:, hs], k16(2)[:, hs], k16(3)[:, hs], op=ALU.mult),
                waits=[(AC, S.val(AC))], inc=DV)
            S.op("vector", lambda e, hs=hs: e.tensor_scalar(
                k16(4)[:, hs], k16(4)[:, hs], -1.0, None, op0=ALU.mult),
                inc=DV)
            kn_dep = (DV, S.val(DV))
            kraw = scr(mg)
            for hh in range(8):
                h = mg * 8 + hh
                S.op("scalar", lambda e, kraw=kraw, hh=hh, h=h, rot=rot:
                     nc.scalar.activation(
                         s_kn[:, rot * D + h * DK: rot * D + (h + 1) * DK],
                         kraw[:, hh * DK:(hh + 1) * DK], AF.Identity,
                         scale=k16(3)[:, h: h + 1], bias=k16(4)[:, h: h + 1]),
                     waits=[kn_dep] + (rd(("kn", rot)) if hh == 0 and mg == 0
                                       else []),
                     inc=AC, fence=False)
            if mg == 1:
                kn_ready_m[tt] = (AC, S.val(AC))

        def emit_transp(tt):
            rot = tt % 2
            mms = []
            for hp in range(8):
                mms.append(lambda e, hp=hp, rot=rot:
                           nc.tensor.transpose(
                               pG[:, hp * 128:(hp + 1) * 128],
                               s_kn[:, rot * D + hp * 128: rot * D + (hp + 1) * 128],
                               s_ident[:]))
            mm_group(pG, mms, extra_waits=[kn_ready_m[tt], consts_ld])
            bank_read(pG, "vector",
                      lambda e, rot=rot: e.tensor_copy(
                          s_knT[:, rot * KC * 128:(rot + 1) * KC * 128], pG[:]),
                      extra_waits=rd(("knT", rot)), fence=False)
            knT_ready_m[tt] = (DV, S.val(DV))

        def emit_retnet(tt):
            rot = tt % 2
            kn_ready = kn_ready_m[tt]
            knT_ready = knT_ready_m[tt]
            v_ready = v_ready_m[tt]
            if plmaj < 4:
                return
            for h in range(H):
                S.op("scalar", lambda e, h=h, rot=rot: nc.scalar.activation(
                    s_kp[:, h * DK:(h + 1) * DK],
                    s_kn[:, rot * D + h * DK: rot * D + (h + 1) * DK],
                    AF.Copy, scale=s_kpow[:, h: h + 1]),
                    waits=[kn_ready] + (rd("kp") if h == 0 else []), inc=AC,
                    fence=False)
            kp_ready = (AC, S.val(AC))
            if phase_limit < 42:
                return
            par = (g * SUB + tt) % 2
            npar = 1 - par
            sdec_marks = {}

            def emit_st(hp):
                bkA, bkB = (pE, pF) if hp % 2 == 0 else (pA, pB)
                h0, h1 = 2 * hp, 2 * hp + 1
                rot2 = hp % 2
                for li, h, bank in ((0, h0, bkA), (1, h1, bkB)):
                    base = 64 * li
                    mm_group(bank, [lambda e, base=base, hp=hp, rot=rot, tt=tt, bank=bank:
                                    nc.tensor.matmul(
                                        bank[0:128, 0:128],
                                        s_knT[base:base + 64,
                                              rot * KC * 128 + hp * 128:
                                              rot * KC * 128 + (hp + 1) * 128],
                                        s_qR[base:base + 64,
                                             hp * G + tt * 128: hp * G + (tt + 1) * 128],
                                        start=True, stop=True)],
                             extra_waits=[knT_ready, qT_ready])
                for li, bank in ((0, bkA), (1, bkB)):
                    bank_read(bank, "vector",
                              lambda e, bank=bank, hp=hp, li=li, rot2=rot2:
                              e.tensor_tensor(
                                  s_sdec[:, rot2 * 256 + li * 128:
                                         rot2 * 256 + (li + 1) * 128],
                                  bank[:, 0:128],
                                  s_dmat[:, hp * 256 + li * 128:
                                         hp * 256 + (li + 1) * 128],
                                  op=ALU.mult),
                              extra_waits=[prelude_ac], fence=False)
                sdec_marks[hp] = (DV, S.val(DV))

            def emit_rest(hp):
                bkA, bkB = (pE, pF) if hp % 2 == 0 else (pA, pB)
                h0, h1 = 2 * hp, 2 * hp + 1
                rot2 = hp % 2
                sdec_done = sdec_marks[hp]
                if phase_limit < 43:
                    return
                mms = []
                for li, h in ((0, h0), (1, h1)):
                    base = 64 * li
                    mms.append(lambda e, li=li, h=h, base=base, rot=rot, rot2=rot2, bank=bkA:
                               nc.tensor.matmul(
                                   bank[base:base + 64, 256:384],
                                   s_v[:, rot * D + h * DK: rot * D + (h + 1) * DK],
                                   s_sdec[:, rot2 * 256 + li * 128:
                                          rot2 * 256 + (li + 1) * 128],
                                   start=True, stop=False))
                    mms.append(lambda e, li=li, base=base, hp=hp, par=par, bank=bkA, tt=tt:
                               nc.tensor.matmul(
                                   bank[base:base + 64, 256:384],
                                   sbf_sl(par, hp)[base:base + 64, :],
                                   s_qT[base:base + 64,
                                        hp * G + tt * 128: hp * G + (tt + 1) * 128],
                                   start=False, stop=True))
                    if phase_limit >= 44:
                        mms.append(lambda e, li=li, h=h, base=base, rot=rot, bank=bkA:
                                   nc.tensor.matmul(
                                       bank[base:base + 64, 384:448],
                                       s_kp[:, h * DK:(h + 1) * DK],
                                       s_v[:, rot * D + h * DK: rot * D + (h + 1) * DK],
                                       start=True, stop=True))
                extra = [sdec_done, v_ready, kp_ready]
                if state_zeroed is not None:
                    extra.append(state_zeroed)
                mm_group(bkA, mms, extra_waits=extra)
                bank_read(bkA, "vector",
                          lambda e, bank=bkA, hp=hp, tt=tt: e.tensor_copy(
                              s_yT[:, hp * G + tt * 128: hp * G + (tt + 1) * 128],
                              bank[:, 256:384]),
                          extra_waits=rd("yT") if (hp == 0 and tt == 0) else [],
                          fence=False)
                if phase_limit < 44:
                    return
                bank_read(bkA, "vector",
                          lambda e, bank=bkA, hp=hp, par=par, npar=npar:
                          e.scalar_tensor_tensor(
                              sf_sl(npar, hp), sf_sl(par, hp),
                              s_gCp[:, hp: hp + 1], bank[:, 384:448],
                              op0=ALU.mult, op1=ALU.add),
                          extra_waits=[pair_done], fence=False)
                S.op("scalar", lambda e, hp=hp, npar=npar: nc.scalar.activation(
                    sbf_sl(npar, hp), sf_sl(npar, hp), AF.Copy),
                    waits=[(DV, S.val(DV))], inc=AC, fence=False)

            emit_st(0)
            for hp in range(8):
                if hp + 1 < 8:
                    emit_st(hp + 1)
                emit_rest(hp)
            note_read(("kn", rot), AC)
            note_read(("v", rot), PE)
            note_read(("knT", rot), PE)
            note_read("kp", PE)

        emit_kv(0)
        emit_qpass()
        for tt in range(TT):
            emit_transp(tt)
            if tt + 1 < TT:
                emit_kv(tt + 1)
            emit_retnet(tt)
        note_read("state", AC)
        note_read("big", PE)
        yT_ready = (DV, S.val(DV))
        state_bf_ready = (AC, S.val(AC))

        if plmaj < 5:
            continue
        if bname(pC) in bank_free:
            S.op("tensor", lambda e: e.nop(), waits=[bank_free[bname(pC)],
                                                     bank_free[bname(pD)]])
        for mc in range(KC):
            bank = pA if mc % 2 == 0 else pB
            mms = []
            for kc in range(KC):
                mms.append(lambda e, kc=kc, mc=mc, bank=bank:
                           nc.tensor.matmul(bank[:], wo_sl(kc, mc * 128, 128),
                                            s_yT[:, kc * G:(kc + 1) * G],
                                            start=(kc == 0), stop=(kc == KC - 1)))
            mm_group(bank, mms, extra_waits=[wo_ld, yT_ready])
            bank_read(bank, "vector",
                      lambda e, mc=mc, bank=bank: e.tensor_tensor(
                          s_xres[:, mc * G:(mc + 1) * G], bank[:],
                          s_xw[:, mc * G:(mc + 1) * G], op=ALU.add),
                      extra_waits=rd("xres") if mc == 0 else [], fence=False)
            xres_mc = (DV, S.val(DV))
            S.op("scalar", lambda e, mc=mc: nc.scalar.activation(
                s_act[:, mc * G:(mc + 1) * G], s_xres[:, mc * G:(mc + 1) * G],
                AF.Square),
                waits=[xres_mc] + (rd("act") if mc == 0 else []), inc=AC,
                fence=False)
            sq_mc = (AC, S.val(AC))
            S.op("tensor", lambda e, mc=mc: nc.tensor.matmul(
                pC[0:1, 0:G], s_onescol[:], s_xres[:, mc * G:(mc + 1) * G],
                start=(mc == 0), stop=(mc == KC - 1)),
                waits=[xres_mc] + ([consts_ld] if mc == 0 else []),
                inc=PE if mc == KC - 1 else None)
            S.op("tensor", lambda e, mc=mc: nc.tensor.matmul(
                pD[0:1, 0:G], s_onescol[:], s_act[:, mc * G:(mc + 1) * G],
                start=(mc == 0), stop=(mc == KC - 1)),
                waits=[sq_mc], inc=PE if mc == KC - 1 else None)
        bank_full[bname(pC)] = (PE, S.val(PE))
        bank_full[bname(pD)] = (PE, S.val(PE))
        note_read("xw", DV)
        note_read("r2", PE)
        xres_ready = (DV, S.val(DV))
        note_read("act", PE)
        bank_read(pC, "vector", lambda e: e.tensor_scalar(
            lnr(0), pC[0:1, 0:G], 1.0 / D, None, op0=ALU.mult),
            fence=False)
        bank_read(pD, "vector", lambda e: e.tensor_scalar(
            lnr(1), pD[0:1, 0:G], 1.0 / D, None, op0=ALU.mult),
            fence=False)
        S.op("vector", lambda e: e.tensor_tensor(
            lnr(2), lnr(0), lnr(0), op=ALU.mult), inc=DV)
        S.op("vector", lambda e: e.tensor_tensor(
            lnr(1), lnr(1), lnr(2), op=ALU.subtract), inc=DV)
        S.op("scalar", lambda e: nc.scalar.activation(
            lnr(1), lnr(1), AF.Ln, bias=s_eps[0:1, :]),
            waits=[(DV, S.val(DV))], inc=AC)
        S.op("scalar", lambda e: nc.scalar.activation(
            lnr(1), lnr(1), AF.Exp, scale=-0.5), inc=AC)
        S.op("vector", lambda e: e.tensor_copy(lnb(0), lnr(1)),
             waits=[(AC, S.val(AC))], inc=DV)
        S.op("vector", lambda e: e.scalar_tensor_tensor(
            lnr(2), lnr(0), -1.0, lnr(1), op0=ALU.mult, op1=ALU.mult), inc=DV)
        S.op("vector", lambda e: e.tensor_copy(lnb(1), lnr(2)), inc=DV)
        ab_ready = (DV, S.val(DV))
        mm_group(pC, [lambda e: nc.tensor.matmul(
            pC[:, 0:G], s_ones1[:], lnb(0), start=True, stop=True)],
            extra_waits=[ab_ready])
        mm_group(pD, [lambda e: nc.tensor.matmul(
            pD[:, 0:G], s_ones1[:], lnb(1), start=True, stop=True)],
            extra_waits=[ab_ready])
        x1_kc_marks = []
        for kc in range(KC):
            S.op("vector", lambda e, kc=kc: e.tensor_tensor(
                scr(kc % 2), s_xres[:, kc * G:(kc + 1) * G], pC[:, 0:G],
                op=ALU.mult),
                waits=([bank_full[bname(pC)], bank_full[bname(pD)]]
                       if kc == 0 else ()), inc=DV, fence=False)
            if kc >= 1:
                S.op("vector", lambda e, kc=kc: e.tensor_tensor(
                    s_x1[:, (kc - 1) * G: kc * G], scr((kc - 1) % 2), pD[:, 0:G],
                    op=ALU.add),
                    waits=rd("x1") if kc == 1 else (), inc=DV, fence=False)
                x1_kc_marks.append((DV, S.val(DV)))
        S.op("vector", lambda e: e.tensor_tensor(
            s_x1[:, (KC - 1) * G: KC * G], scr((KC - 1) % 2), pD[:, 0:G],
            op=ALU.add), inc=DV)
        x1_kc_marks.append((DV, S.val(DV)))
        x1_ready = (DV, S.val(DV))
        bank_free[bname(pC)] = (DV, S.val(DV))
        bank_free[bname(pD)] = (DV, S.val(DV))


        if plmaj < 6:
            continue
        for tt in range(TT):
            mms = []
            for kc in range(KC):
                mms.append(lambda e, kc=kc, tt=tt: nc.tensor.matmul(
                    pE[:, 0:16],
                    s_x1[:, kc * G + tt * 128: kc * G + (tt + 1) * 128],
                    s_rt[:, kc * E:(kc + 1) * E],
                    start=(kc == 0), stop=(kc == KC - 1)))
            mm_group(pE, mms, extra_waits=[x1_ready, pre_pd])
            bank_read(pE, "vector", lambda e: e.tensor_tensor(
                s_lg[:, 0:16], pE[:, 0:16], s_brep[:], op=ALU.add),
                extra_waits=[consts_ld], fence=False)
            S.op("vector", lambda e: e.tensor_reduce(
                sm(0), s_lg[:, 0:16], axis=AX.X, op=ALU.max, negate=True), inc=DV)
            S.op("scalar", lambda e: nc.scalar.activation(
                s_lg[:, 16:32], s_lg[:, 0:16], AF.Exp, bias=sm(0),
                accum_out=sm(1)), waits=[(DV, S.val(DV))], inc=AC)
            S.op("vector", lambda e: e.reciprocal(sm(2), sm(1)),
                 waits=[(AC, S.val(AC))], inc=DV, fence=False)
            S.op("vector", lambda e: e.tensor_scalar(
                s_lg[:, 32:48], s_lg[:, 16:32], sm(2), None, op0=ALU.mult), inc=DV)
            S.op("vector", lambda e: e.max(out=s_top8[:], in_=s_lg[:, 32:48]), inc=DV)
            S.op("vector", lambda e, tt=tt: e.scalar_tensor_tensor(
                s_w[:, tt * E:(tt + 1) * E], s_lg[:, 32:48], s_top8[:, 3:4],
                s_lg[:, 32:48], op0=ALU.is_ge, op1=ALU.mult),
                waits=rd("w") if tt == 0 else [], inc=DV)
            S.op("vector", lambda e, tt=tt: e.tensor_copy(
                s_wbf[:], s_w[:, tt * E:(tt + 1) * E]), inc=DV)
            w_ready = (DV, S.val(DV))
            mm_group(pG, [lambda e: nc.tensor.transpose(
                pG[0:16, 0:128], s_wbf[:, 0:16], s_ident[:])],
                extra_waits=[w_ready])
            bank_read(pG, "vector", lambda e, tt=tt: e.tensor_copy(
                s_wT[0:16, tt * 128:(tt + 1) * 128], pG[0:16, 0:128]),
                extra_waits=rd("wT") if tt == 0 else [], fence=False)
            wT_ready = (DV, S.val(DV))
            for hh in range(2):
                bank = pE if hh == 0 else pF
                mm_group(bank, [lambda e, tt=tt, hh=hh, bank=bank: nc.tensor.matmul(
                    bank[:, 0:512], s_wT[0:16, tt * 128:(tt + 1) * 128],
                    s_db[0:16, hh * 512:(hh + 1) * 512], start=True, stop=True)],
                    extra_waits=[wT_ready, pre_pd])
                bank_read(bank, "vector", lambda e, tt=tt, hh=hh, bank=bank:
                          e.tensor_copy(
                              s_acc[:, tt * D + hh * 512: tt * D + (hh + 1) * 512],
                              bank[:, 0:512]),
                          extra_waits=rd("acc") if (tt == 0 and hh == 0) else [],
                          fence=False)
        acc_init = (DV, S.val(DV))
        w_all = (DV, S.val(DV))

        for tt in range(TT):
            mms = []
            for kc in range(KC):
                mms.append(lambda e, kc=kc, tt=tt: nc.tensor.transpose(
                    pG[:, kc * 128:(kc + 1) * 128],
                    s_x1[:, kc * G + tt * 128: kc * G + (tt + 1) * 128],
                    s_ident[:]))
            mm_group(pG, mms, extra_waits=[x1_kc_marks[KC - 1], consts_ld])
            bank_read(pG, "vector",
                      lambda e, tt=tt: e.tensor_copy(x1tok_sl(tt), pG[:]),
                      fence=False)
        x1tok_ready = (DV, S.val(DV))


        if plmaj < 7:
            continue
        scr_reader = {}
        for ex in range(E):
            if first:
                S.op("gpsimd", lambda e, ex=ex: e.dma_start(
                    out=s_big[:, 0:KC * GU].rearrange("p (kc n) -> p kc n", kc=KC),
                    in_=gu_proj[ex].rearrange("(kc p) n -> p kc n", p=128)),
                    waits=rd("big") + [wq_wb], inc=PD)
                gu_ld = (PD, S.val(PD))
                S.op("gpsimd", lambda e, ex=ex: e.dma_start(
                    out=gu_cache[ex].rearrange("(kc p) n -> p kc n", p=128),
                    in_=s_big[:, 0:KC * GU].rearrange("p (kc n) -> p kc n", kc=KC)),
                    waits=[gu_ld], inc=PD)
                S.op("gpsimd", lambda e, ex=ex: e.dma_start(
                    out=s_r2[:, 0:KC * D].rearrange("p (kc n) -> p kc n", kc=KC),
                    in_=down_proj[ex].rearrange("(kc p) n -> p kc n", p=128)),
                    waits=rd("r2") + [wo_wb], inc=PD)
                dn_ld = (PD, S.val(PD))
                S.op("gpsimd", lambda e, ex=ex: e.dma_start(
                    out=down_cache[ex].rearrange("(kc p) n -> p kc n", p=128),
                    in_=s_r2[:, 0:KC * D].rearrange("p (kc n) -> p kc n", kc=KC)),
                    waits=[dn_ld], inc=PD)
                if ex == E - 1:
                    gu_wb = (PD, S.val(PD))
            else:
                S.op("gpsimd", lambda e, ex=ex: e.dma_start(
                    out=s_big[:, 0:KC * GU].rearrange("p (kc n) -> p kc n", kc=KC),
                    in_=gu_cache[ex].rearrange("(kc p) n -> p kc n", p=128)),
                    waits=rd("big") + [gu_wb], inc=PD)
                gu_ld = (PD, S.val(PD))
                S.op("gpsimd", lambda e, ex=ex: e.dma_start(
                    out=s_r2[:, 0:KC * D].rearrange("p (kc n) -> p kc n", kc=KC),
                    in_=down_cache[ex].rearrange("(kc p) n -> p kc n", p=128)),
                    waits=rd("r2") + [gu_wb], inc=PD)
                dn_ld = (PD, S.val(PD))

            for j in range(KC):
                gbank = pA if j % 2 == 0 else pB
                ubank = pC if j % 2 == 0 else pD
                mms = []
                for kc in range(KC):
                    mms.append(lambda e, kc=kc, j=j, gbank=gbank: nc.tensor.matmul(
                        gbank[:], gu_sl(kc, j * 128, 128), s_x1[:, kc * G:(kc + 1) * G],
                        start=(kc == 0), stop=(kc == KC - 1)))
                mm_group(gbank, mms, extra_waits=[gu_ld, x1_ready])
                mms = []
                for kc in range(KC):
                    mms.append(lambda e, kc=kc, j=j, ubank=ubank: nc.tensor.matmul(
                        ubank[:], gu_sl(kc, NI + j * 128, 128),
                        s_x1[:, kc * G:(kc + 1) * G],
                        start=(kc == 0), stop=(kc == KC - 1)))
                mm_group(ubank, mms, extra_waits=[gu_ld, x1_ready])
                svs = scr(0 + (j % 2))
                bank_read(gbank, "scalar",
                          lambda e, gbank=gbank, ex=ex, j=j, svs=svs:
                          nc.scalar.activation(
                              svs, gbank[:], AF.Silu, scale=ALPHA,
                              bias=s_gubsc[:, ex * 8 + j: ex * 8 + j + 1]),
                          extra_waits=[prelude_dv] +
                          scr_reader.get(("sv", j % 2), []), fence=False)
                sv_done = (AC, S.val(AC))
                S.op("vector", lambda e, svs=svs: e.tensor_scalar(
                    svs, svs, SV_CLIP, 1.0 / ALPHA, op0=ALU.min, op1=ALU.mult),
                    waits=[sv_done], inc=DV, fence=False)
                u2 = scr(2 + (j % 2))
                bank_read(ubank, "vector",
                          lambda e, ubank=ubank, ex=ex, j=j, u2=u2: e.tensor_scalar(
                              u2, ubank[:],
                              s_gubu1[:, ex * 8 + j: ex * 8 + j + 1],
                              LIMIT + 1.0, op0=ALU.add, op1=ALU.min),
                          extra_waits=[prelude_dv] +
                          scr_reader.get(("u", j % 2), []), fence=False)
                S.op("vector", lambda e, j=j, u2=u2, svs=svs: e.scalar_tensor_tensor(
                    s_act[:, j * G:(j + 1) * G], u2, -(LIMIT - 1.0), svs,
                    op0=ALU.max, op1=ALU.mult),
                    waits=(rd("act") if (j == 0 and ex == 0) else []), inc=DV)
                scr_reader[("sv", j % 2)] = [(DV, S.val(DV))]
                scr_reader[("u", j % 2)] = [(DV, S.val(DV))]
            act_ready = (DV, S.val(DV))
            note_read("big", PE)

            for tt in range(TT):
                w0 = bank_free.get(bname(pE))
                w1 = bank_free.get(bname(pF))
                mms = []
                for kc in range(KC):
                    for hh, bank in ((0, pE), (1, pF)):
                        mms.append(lambda e, kc=kc, tt=tt, hh=hh, bank=bank:
                                   nc.tensor.matmul(
                                       bank[:],
                                       s_act[:, kc * G + tt * 128: kc * G + (tt + 1) * 128],
                                       dn_sl(kc, hh * 512, 512),
                                       start=(kc == 0), stop=(kc == KC - 1)))
                ew = ([act_ready, dn_ld, acc_init] + ([w0] if w0 else [])
                      + ([w1] if w1 else []))
                for i, fn in enumerate(mms):
                    S.op("tensor", fn, waits=tuple(ew) if i == 0 else (),
                         inc=PE if i == len(mms) - 1 else None)
                bank_full[bname(pE)] = (PE, S.val(PE))
                bank_full[bname(pF)] = (PE, S.val(PE))
                for hh, bank in ((0, pE), (1, pF)):
                    bank_read(bank, "vector",
                              lambda e, tt=tt, hh=hh, ex=ex, bank=bank:
                              e.scalar_tensor_tensor(
                                  s_acc[:, tt * D + hh * 512: tt * D + (hh + 1) * 512],
                                  bank[:], s_w[:, tt * E + ex: tt * E + ex + 1],
                                  s_acc[:, tt * D + hh * 512: tt * D + (hh + 1) * 512],
                                  op0=ALU.mult, op1=ALU.add),
                              extra_waits=[w_all], fence=False)
            note_read("r2", PE)
        acc_ready = (DV, S.val(DV))
        note_read("w", DV)
        note_read("wT", PE)

        if plmaj < 8:
            continue
        for tt in range(TT):
            rot = tt % 2
            S.op("vector", lambda e, tt=tt, rot=rot: e.tensor_tensor(
                s_r[:, rot * D:(rot + 1) * D], s_acc[:, tt * D:(tt + 1) * D],
                x1tok_sl(tt), op=ALU.add),
                waits=[acc_ready, x1tok_ready] + rd(("r", rot)), inc=DV,
                fence=False)
            S.op("vector", lambda e, rot=rot: e.tensor_reduce(
                sm(3), s_r[:, rot * D:(rot + 1) * D], axis=AX.X, op=ALU.add,
                negate=True), inc=DV)
            S.op("vector", lambda e: e.tensor_scalar(
                sm(3), sm(3), 1.0 / D, None, op0=ALU.mult), inc=DV)
            r_done = (DV, S.val(DV))
            S.op("scalar", lambda e, rot=rot: nc.scalar.activation(
                s_r[:, (1 - rot) * D:(2 - rot) * D], s_r[:, rot * D:(rot + 1) * D],
                AF.Square, bias=sm(3), accum_out=sm(4)), waits=[r_done], inc=AC,
                fence=False)
            S.op("scalar", lambda e: nc.scalar.activation(
                sm(5), sm(4), AF.Ln, scale=1.0 / D, bias=s_eps[:]), inc=AC)
            S.op("scalar", lambda e: nc.scalar.activation(
                sm(5), sm(5), AF.Exp, scale=-0.5), inc=AC)
            S.op("vector", lambda e, tt=tt, rot=rot: e.tensor_scalar(
                s_stage[:, rot * D:(rot + 1) * D], s_r[:, rot * D:(rot + 1) * D],
                sm(3), sm(5), op0=ALU.add, op1=ALU.mult),
                waits=[(AC, S.val(AC))] + rd(("stage", rot)), inc=DV)
            note_read(("r", rot), DV)
            S.op("sync", lambda e, b=b, t0=t0, tt=tt, rot=rot: e.dma_start(
                out=out[b, t0 + tt * 128: t0 + (tt + 1) * 128, :],
                in_=s_stage[:, rot * D:(rot + 1) * D]),
                waits=[(DV, S.val(DV))], inc=SD)
            note_read(("stage", rot), SD)
        note_read("acc", DV)
        note_read("xres", DV)
        note_read("x1", PE)
        note_read("qT", PE)
        note_read("qR", PE)
        note_read("yT", PE)

    if dump:
        dump_list = [s_xw, s_qR, s_qT, s_kn, s_knT, s_v, s_yT, s_xres, s_x1,
                     s_w, s_acc, s_sf, s_sbf, s_sdec, s_kp, s_stage, s_r,
                     s_llg, s_gpow, s_dmat, s_kpow, s_gCp, s_lgbc, s_scr,
                     s_act]
        allw = [(PE, S.val(PE)), (DV, S.val(DV)), (AC, S.val(AC)),
                (SD, S.val(SD)), (PD, S.val(PD))]
        for t in dump_list:
            nm = t.name if isinstance(t.name, str) else t.name()
            dt = nc.dram_tensor("dbg_" + nm, list(t.shape), t.dtype,
                                kind="ExternalOutput").ap()
            S.op("sync", lambda e, t=t, dt=dt: e.dma_start(out=dt, in_=t[:]),
                 waits=allw, inc=SD)

    S.op("sync", lambda e: e.nop(), waits=[(SD, S.val(SD))])

    with nc.Block() as block:
        S.emit_all(block)
    return nc


def _consts():
    i_idx = np.arange(128, dtype=np.float32)
    idxT = np.maximum(i_idx[None, :] - i_idx[:, None], 0.0).astype(np.float32)
    maskT = np.where(i_idx[None, :] >= i_idx[:, None], 0.0, NEG_BIG).astype(np.float32)
    e1 = np.broadcast_to(i_idx[None, :] + 1.0, (128, 128)).astype(np.float32).copy()
    c2 = (127.0 - i_idx)[:, None].astype(np.float32).copy()
    ident = np.eye(128, dtype=np.float32).astype(bfdt)
    ones1 = np.ones((1, 128), dtype=np.float32).astype(bfdt)
    onescol = np.ones((128, 1), dtype=np.float32).astype(bfdt)
    return dict(c_idxT=idxT, c_maskT=maskT, c_e1=e1, c_c2=c2, c_ident=ident,
                c_ones1=ones1, c_onescol=onescol)


def kernel(L0_attn_x, L0_attn_W_qkv, L0_attn_W_o, L0_attn_log_gamma,
           L0_ffn_router_weight, L0_ffn_router_bias, L0_ffn_gate_up_proj,
           L0_ffn_gate_up_bias, L0_ffn_down_proj, L0_ffn_down_bias):
    B, L, _ = L0_attn_x.shape
    ncores = 8
    nseq = B // ncores
    nc = build_nc(NSEQ=nseq, L=L, G=512)

    xt = np.ascontiguousarray(np.transpose(np.asarray(L0_attn_x), (0, 2, 1)))
    wqkv_t = np.ascontiguousarray(np.asarray(L0_attn_W_qkv).T)
    wo_t = np.ascontiguousarray(np.asarray(L0_attn_W_o).T)
    router_t = np.ascontiguousarray(np.asarray(L0_ffn_router_weight).T)
    gub_t = np.ascontiguousarray(
        np.asarray(L0_ffn_gate_up_bias).reshape(16, 16, 128).transpose(2, 0, 1)
        .reshape(128, 256))
    consts = _consts()

    base = dict(wqkv_t=wqkv_t, wo_t=wo_t, router_t=router_t,
                log_gamma=np.asarray(L0_attn_log_gamma),
                r_bias=np.asarray(L0_ffn_router_bias), gub_t=gub_t,
                gu_proj=np.asarray(L0_ffn_gate_up_proj),
                down_proj=np.asarray(L0_ffn_down_proj),
                down_bias=np.asarray(L0_ffn_down_bias), **consts)
    in_maps = []
    for c in range(ncores):
        m = dict(base)
        m["x_t"] = np.ascontiguousarray(xt[c * nseq:(c + 1) * nseq])
        in_maps.append(m)

    res = run_bass_kernel_spmd(nc, in_maps, core_ids=list(range(ncores)))
    return np.concatenate([r["out"] for r in res.results], axis=0)


# version 16
# speedup vs baseline: 1.1419x; 1.1419x over previous
import numpy as np
import ml_dtypes
import concourse.bass as bass
import concourse.mybir as mybir
from concourse.bass_utils import run_bass_kernel_spmd

F32 = mybir.dt.float32
BF16 = mybir.dt.bfloat16
AF = mybir.ActivationFunctionType
ALU = mybir.AluOpType
AX = mybir.AxisListType
bfdt = ml_dtypes.bfloat16

D = 1024
H = 16
DK = 64
E = 16
NI = 1024
GU = 2 * NI
C = 128
KC = D // 128
LN_EPS = 1e-5
ALPHA = 1.702
LIMIT = 7.0
NEG_BIG = -1.0e9


class Sched:

    def __init__(self, nc):
        self.nc = nc
        self.ops = []
        self.cnt = {}
        self.sems = {}

    def sem(self, name):
        self.sems[name] = self.nc.alloc_semaphore(name=name)
        self.cnt[name] = 0
        return name

    def op(self, eng, fn, waits=(), inc=None, n=1, fence=True):
        if inc in ("SD", "PD"):
            n = 16
        waits = tuple((s, v) for (s, v) in waits if v > 0)
        for s, v in waits:
            assert v <= self.cnt[s], f"future wait {s} {v} > {self.cnt[s]}"
        if inc is not None:
            self.cnt[inc] += n
        self.ops.append((eng, waits, fn, inc, n, fence))

    def val(self, sem):
        return self.cnt[sem]

    def emit_all(self, block):
        engmap = {
            "tensor": block.tensor, "vector": block.vector,
            "scalar": block.scalar, "sync": block.sync, "gpsimd": block.gpsimd,
        }
        selfsem = {"vector": "DV", "scalar": "AC", "gpsimd": "PL"}
        for name, deco in engmap.items():
            ops = [o for o in self.ops if o[0] == name]

            def make(ops_, name_):
                def body(eng):
                    seen = {}
                    own = selfsem.get(name_)
                    cnt = 0
                    for (_e, waits, fn, inc, n, fence) in ops_:
                        if own is not None and cnt > 0 and fence:
                            if seen.get(own, 0) < cnt:
                                eng.wait_ge(self.sems[own], cnt)
                                seen[own] = cnt
                        for (s, v) in waits:
                            if seen.get(s, 0) < v:
                                eng.wait_ge(self.sems[s], v)
                                seen[s] = v
                        inst = fn(eng)
                        if inc is not None:
                            inst.then_inc(self.sems[inc], n)
                            if inc == own:
                                cnt += n
                return body
            deco(make(ops, name))


def build_nc(NSEQ=2, L=4096, G=512, n_dbg=0, phase_limit=99, dump=False):
    nc = bass.Bass("TRN2", target_bir_lowering=False,
                   detect_race_conditions=False)
    plmaj = 4 if 40 <= phase_limit < 90 else phase_limit
    NCH = (L // G) * NSEQ
    TT = G // 128
    SUB = G // C

    def din(name, shape, dt=F32):
        return nc.dram_tensor(name, list(shape), dt, kind="ExternalInput").ap()

    x_t = din("x_t", [NSEQ, D, L])
    wqkv_t = din("wqkv_t", [D, 3 * D])
    wo_t = din("wo_t", [D, D])
    router_t = din("router_t", [D, E])
    log_gamma = din("log_gamma", [H])
    r_bias = din("r_bias", [E])
    gub_t = din("gub_t", [128, 256])
    gu_proj = din("gu_proj", [E, D, GU])
    down_proj = din("down_proj", [E, NI, D])
    down_bias = din("down_bias", [E, D])
    c_ident = din("c_ident", [128, 128], BF16)
    c_idxT = din("c_idxT", [128, 128])
    c_maskT = din("c_maskT", [128, 128])
    c_e1 = din("c_e1", [128, 128])
    c_c2 = din("c_c2", [128, 1])
    c_ones1 = din("c_ones1", [1, 128], BF16)
    c_onescol = din("c_onescol", [128, 1], BF16)

    out = nc.dram_tensor("out", [NSEQ, L, D], F32, kind="ExternalOutput").ap()
    dbg = None
    if n_dbg:
        dbg = nc.dram_tensor("dbg", [n_dbg, 128, 4096], F32,
                             kind="ExternalOutput").ap()

    wqkv_cache = nc.dram_tensor("wqkv_cache", [D, 3 * D], BF16, kind="Internal").ap()
    wo_cache = nc.dram_tensor("wo_cache", [D, D], BF16, kind="Internal").ap()
    gu_cache = nc.dram_tensor("gu_cache", [E, D, GU], BF16, kind="Internal").ap()
    down_cache = nc.dram_tensor("down_cache", [E, NI, D], BF16, kind="Internal").ap()

    A = nc.alloc_sbuf_tensor
    s_big = A("s_big", [128, KC * 3 * D], BF16)
    wq_sl = lambda kc, m0, w: s_big[:, kc * 3 * D + m0: kc * 3 * D + m0 + w]
    gu_sl = lambda kc, n0, w: s_big[:, kc * GU + n0: kc * GU + n0 + w]
    s_r2 = A("s_r2", [128, KC * D], BF16)
    wo_sl = lambda kc, m0, w: s_r2[:, kc * D + m0: kc * D + m0 + w]
    dn_sl = lambda kc, h0, w: s_r2[:, kc * D + h0: kc * D + h0 + w]

    s_xw = A("s_xw", [128, KC * G], BF16)
    s_qT = A("s_qT", [128, KC * G], BF16)
    s_qR = A("s_qR", [128, KC * G], BF16)
    s_yT = A("s_yT", [128, KC * G], BF16)
    s_xres = A("s_xres", [128, KC * G], BF16)
    x1tok_sl = lambda tt: s_xres[:, tt * D:(tt + 1) * D]
    s_x1 = A("s_x1", [128, KC * G], BF16)
    s_act = A("s_act", [128, KC * G], BF16)
    s_kn = A("s_kn", [128, 2 * D], BF16)
    s_knT = A("s_knT", [128, 2 * KC * 128], BF16)
    s_v = A("s_v", [128, 2 * D], BF16)
    s_kp = A("s_kp", [128, D], BF16)
    s_sdec = A("s_sdec", [128, 2 * 256], BF16)
    s_acc = A("s_acc", [128, TT * D], F32)
    s_stage = A("s_stage", [128, 2 * D], F32)
    s_r = A("s_r", [128, 2 * D], F32)
    s_scr = A("s_scr", [128, 4 * 512], F32)
    s_eps = A("s_eps", [128, 1], F32)
    scr = lambda i, w=512: s_scr[:, i * 512: i * 512 + w]
    s_sf = A("s_sf", [128, 2 * 8 * DK], F32)
    s_sbf = A("s_sbf", [128, 2 * 8 * DK], BF16)
    sf_sl = lambda par, hp: s_sf[:, par * 512 + hp * DK: par * 512 + (hp + 1) * DK]
    sbf_sl = lambda par, hp: s_sbf[:, par * 512 + hp * DK: par * 512 + (hp + 1) * DK]

    s_ident = A("s_ident", [128, 128], BF16)
    s_idxT = A("s_idxT", [128, 128], F32)
    s_maskT = A("s_maskT", [128, 128], F32)
    s_e1 = A("s_e1", [128, 128], F32)
    s_c2 = A("s_c2", [128, 1], F32)
    s_ones1 = A("s_ones1", [1, 128], BF16)
    s_onescol = A("s_onescol", [128, 1], BF16)
    s_dmat = A("s_dmat", [128, 8 * 256], BF16)
    s_gpow = A("s_gpow", [128, 8 * 128], BF16)
    s_lgbc = A("s_lgbc", [128, H], F32)
    s_brep = A("s_brep", [128, E], F32)
    s_llg = A("s_llg", [128, H], F32)
    s_gC = A("s_gC", [128, H], F32)
    s_gCp = A("s_gCp", [128, 8], F32)
    s_llgp = A("s_llgp", [128, 8], F32)
    s_kpow = A("s_kpow", [128, H], F32)
    s_gub = A("s_gub", [128, 256], F32)
    s_gubsc = A("s_gubsc", [128, 128], F32)
    s_gubu1 = A("s_gubu1", [128, 128], F32)
    s_rt = A("s_rt", [128, KC * E], BF16)
    s_db = A("s_db", [16, D], BF16)
    s_lg = A("s_lg", [128, 16 * 8], F32)
    s_w = A("s_w", [128, TT * E], F32)
    s_wbf = A("s_wbf", [128, E], BF16)
    s_wT = A("s_wT", [16, TT * 128], BF16)
    s_top8 = A("s_top8", [128, 8], F32)
    s_sm = A("s_sm", [128, 64], F32)
    sm = lambda i, w=1: s_sm[:, i: i + w]
    s_ln = A("s_ln", [1, 3 * 512], F32)
    lnr = lambda i: s_ln[0:1, i * 512: (i + 1) * 512]
    s_lnb = A("s_lnb", [1, 2 * 512], BF16)
    lnb = lambda i: s_lnb[0:1, i * 512: (i + 1) * 512]
    s_k16 = A("s_k16", [128, 6 * 16], F32)
    k16 = lambda i: s_k16[:, i * 16: (i + 1) * 16]

    P = nc.alloc_psum_tensor
    pA = P("pA", [128, 512], F32)
    pB = P("pB", [128, 512], F32)
    pC = P("pC", [128, 512], F32)
    pD = P("pD", [128, 512], F32)
    pE = P("pE", [128, 512], F32)
    pF = P("pF", [128, 512], F32)
    pG = P("pG", [128, 1024], BF16)

    S = Sched(nc)
    PE = S.sem("PE"); DV = S.sem("DV"); AC = S.sem("AC")
    SD = S.sem("SD"); PD = S.sem("PD"); PL = S.sem("PL")

    bank_free = {}
    bank_full = {}

    def bname(bank):
        return id(bank)

    def mm_group(bank, mms, extra_waits=()):
        name = bname(bank)
        w = list(extra_waits)
        if name in bank_free:
            w.append(bank_free[name])
        for i, fn in enumerate(mms):
            S.op("tensor", fn, waits=tuple(w) if i == 0 else (),
                 inc=PE if i == len(mms) - 1 else None)
        bank_full[name] = (PE, S.val(PE))

    def bank_read(bank, eng, fn, extra_waits=(), inc=None, fence=True):
        name = bname(bank)
        w = list(extra_waits)
        if name in bank_full:
            w.append(bank_full[name])
        semmap = {"vector": DV, "scalar": AC, "gpsimd": PL}
        s = semmap[eng] if inc is None else inc
        S.op(eng, fn, waits=tuple(w), inc=s, fence=fence)
        bank_free[name] = (s, S.val(s))

    last_read = {}

    def note_read(key, sem):
        last_read[key] = (sem, S.val(sem))

    def rd(key):
        return [last_read[key]] if key in last_read else []

    for dst, src in [(s_ident, c_ident), (s_idxT, c_idxT), (s_maskT, c_maskT),
                     (s_e1, c_e1), (s_c2, c_c2), (s_ones1, c_ones1),
                     (s_onescol, c_onescol), (s_gub, gub_t)]:
        S.op("sync", lambda e, d=dst, s=src: e.dma_start(out=d[:], in_=s),
             inc=SD)
    S.op("sync", lambda e: e.dma_start(
        out=s_lgbc[:], in_=log_gamma.unsqueeze(0).to_broadcast([128, H])), inc=SD)
    S.op("sync", lambda e: e.dma_start(
        out=s_brep[:], in_=r_bias.unsqueeze(0).to_broadcast([128, E])), inc=SD)
    consts_ld = (SD, S.val(SD))
    S.op("gpsimd", lambda e: e.dma_start(
        out=s_rt[:].rearrange("p (kc e) -> p kc e", kc=KC),
        in_=router_t.rearrange("(kc p) e -> p kc e", p=128)), inc=PD)
    S.op("gpsimd", lambda e: e.dma_start(out=s_db[:], in_=down_bias), inc=PD)
    pre_pd = (PD, S.val(PD))

    S.op("scalar", lambda e: nc.scalar.activation(
        s_llg[:], s_lgbc[:], AF.Sigmoid), waits=[consts_ld], inc=AC)
    S.op("scalar", lambda e: nc.scalar.activation(
        s_llg[:], s_llg[:], AF.Ln), inc=AC)
    S.op("scalar", lambda e: nc.scalar.activation(
        s_gC[:], s_llg[:], AF.Exp, scale=float(C)), inc=AC)
    llg_done = (AC, S.val(AC))
    S.op("vector", lambda e: e.tensor_scalar(
        s_kpow[:], s_llg[:], s_c2[:], None, op0=ALU.mult),
        waits=[llg_done], inc=DV)
    S.op("scalar", lambda e: nc.scalar.activation(
        s_kpow[:], s_kpow[:], AF.Exp), waits=[(DV, S.val(DV))], inc=AC)
    S.op("vector", lambda e: e.tensor_copy(
        s_llgp[0:64, :], s_llg[0:64, :].rearrange("p (a two) -> p a two", two=2)[:, :, 0]),
        waits=[llg_done], inc=DV)
    S.op("vector", lambda e: e.tensor_copy(
        s_llgp[64:128, :], s_llg[64:128, :].rearrange("p (a two) -> p a two", two=2)[:, :, 1]),
        inc=DV)
    S.op("vector", lambda e: e.tensor_copy(
        s_gCp[0:64, :], s_gC[0:64, :].rearrange("p (a two) -> p a two", two=2)[:, :, 0]),
        inc=DV)
    S.op("vector", lambda e: e.tensor_copy(
        s_gCp[64:128, :], s_gC[64:128, :].rearrange("p (a two) -> p a two", two=2)[:, :, 1]),
        inc=DV)
    pair_done = (DV, S.val(DV))
    for c in range(8):
        S.op("scalar", lambda e, c=c: nc.scalar.activation(
            s_gpow[:, c * 128:(c + 1) * 128], s_e1[:], AF.Exp,
            scale=s_llgp[:, c: c + 1]), waits=[pair_done], inc=AC)
    for h in range(H):
        hp, lo = divmod(h, 2)
        col = hp * 256 + lo * 128
        S.op("vector", lambda e, h=h: e.scalar_tensor_tensor(
            scr(0, 128), s_idxT[:], s_llg[:, h: h + 1], s_maskT[:],
            op0=ALU.mult, op1=ALU.add), waits=[llg_done, consts_ld], inc=DV)
        S.op("scalar", lambda e, col=col: nc.scalar.activation(
            s_dmat[:, col: col + 128], scr(0, 128), AF.Exp),
            waits=[(DV, S.val(DV))], inc=AC)
        S.op("vector", lambda e: e.engine_nop(), waits=[(AC, S.val(AC))], inc=DV)
    S.op("vector", lambda e: e.tensor_scalar(
        s_gubsc[:].rearrange("p (e j) -> p e j", e=E),
        s_gub[:].rearrange("p (e n) -> p e n", e=E)[:, :, 0:8],
        ALPHA, None, op0=ALU.mult), waits=[consts_ld], inc=DV)
    S.op("vector", lambda e: e.tensor_scalar(
        s_gubu1[:].rearrange("p (e j) -> p e j", e=E),
        s_gub[:].rearrange("p (e n) -> p e n", e=E)[:, :, 8:16],
        1.0, None, op0=ALU.add), waits=[consts_ld], inc=DV)
    S.op("vector", lambda e: e.memset(s_sf[:], 0.0), inc=DV)
    S.op("vector", lambda e: e.memset(s_eps[:], LN_EPS), inc=DV)
    S.op("vector", lambda e: e.memset(s_sbf[:], 0.0), inc=DV)
    prelude_dv = (DV, S.val(DV))
    prelude_ac = (AC, S.val(AC))


    C7 = LIMIT
    SV_CLIP = float(ALPHA * LIMIT)

    dbg_next = [0]

    def dbg_dump(src_ap, waits):
        if dbg is None or dbg_next[0] >= n_dbg:
            return
        i = dbg_next[0]
        dbg_next[0] += 1
        S.op("sync", lambda e: e.dma_start(out=dbg[i], in_=src_ap),
             waits=waits, inc=SD)

    for ch in range(NCH):
        b, g = divmod(ch, L // G)
        t0 = g * G
        first = (ch == 0)
        seq_start = (g == 0)

        S.op("gpsimd", lambda e, b=b, t0=t0: e.dma_start(
            out=s_xw[:].rearrange("p (kc t) -> p kc t", kc=KC),
            in_=x_t[b, :, t0:t0 + G].rearrange("(kc p) t -> p kc t", p=128)),
            waits=rd("xw") + ([pre_pd] if first else []), inc=PD)
        xw_ld = (PD, S.val(PD))

        if first:
            S.op("gpsimd", lambda e: e.dma_start(
                out=s_big[:].rearrange("p (kc m) -> p kc m", kc=KC),
                in_=wqkv_t.rearrange("(kc p) m -> p kc m", p=128)), inc=PD)
            wq_ld = (PD, S.val(PD))
            S.op("gpsimd", lambda e: e.dma_start(
                out=wqkv_cache.rearrange("(kc p) m -> p kc m", p=128),
                in_=s_big[:].rearrange("p (kc m) -> p kc m", kc=KC)),
                waits=[wq_ld], inc=PD)
            wq_wb = (PD, S.val(PD))
            S.op("gpsimd", lambda e: e.dma_start(
                out=s_r2[:].rearrange("p (kc m) -> p kc m", kc=KC),
                in_=wo_t.rearrange("(kc p) m -> p kc m", p=128)), inc=PD)
            wo_ld = (PD, S.val(PD))
            S.op("gpsimd", lambda e: e.dma_start(
                out=wo_cache.rearrange("(kc p) m -> p kc m", p=128),
                in_=s_r2[:].rearrange("p (kc m) -> p kc m", kc=KC)),
                waits=[wo_ld], inc=PD)
            wo_wb = (PD, S.val(PD))
        else:
            S.op("sync", lambda e: e.dma_start(
                out=s_big[:].rearrange("p (kc m) -> p kc m", kc=KC),
                in_=wqkv_cache.rearrange("(kc p) m -> p kc m", p=128)),
                waits=rd("big") + [wq_wb], inc=SD)
            wq_ld = (SD, S.val(SD))
            S.op("sync", lambda e: e.dma_start(
                out=s_r2[:].rearrange("p (kc m) -> p kc m", kc=KC),
                in_=wo_cache.rearrange("(kc p) m -> p kc m", p=128)),
                waits=rd("r2") + [wo_wb], inc=SD)
            wo_ld = (SD, S.val(SD))

        if seq_start and not first:
            S.op("vector", lambda e: e.memset(s_sf[:], 0.0),
                 waits=rd("state"), inc=DV)
            S.op("vector", lambda e: e.memset(s_sbf[:], 0.0), inc=DV)
        state_zeroed = (DV, S.val(DV)) if seq_start else None

        qT_ready = None
        if plmaj < 2:
            continue
        def emit_qpass():
            nonlocal qT_ready
            for mc in range(KC):
                bank = pA if mc % 2 == 0 else pB
                mms = []
                for kc in range(KC):
                    mms.append(lambda e, kc=kc, mc=mc, bank=bank:
                               nc.tensor.matmul(bank[:], wq_sl(kc, mc * 128, 128),
                                                s_xw[:, kc * G:(kc + 1) * G],
                                                start=(kc == 0), stop=(kc == KC - 1)))
                mm_group(bank, mms, extra_waits=[wq_ld, xw_ld])
                S.op("vector", lambda e, mc=mc, bank=bank: e.tensor_copy(
                    s_qR[:, mc * G:(mc + 1) * G], bank[:]),
                    waits=[bank_full[bname(bank)]] + (rd("qR") if mc == 0 else []),
                    inc=DV, fence=False)
                S_gp = s_gpow[:, mc * 128:(mc + 1) * 128]
                bank_read(bank, "vector",
                          lambda e, mc=mc, bank=bank, S_gp=S_gp: e.tensor_tensor(
                              s_qT[:, mc * G:(mc + 1) * G].rearrange(
                                  "p (tt c) -> p tt c", c=128),
                              bank[:].rearrange("p (tt c) -> p tt c", c=128),
                              S_gp.unsqueeze(1).to_broadcast([128, TT, 128]),
                              op=ALU.mult),
                          extra_waits=[prelude_ac] + rd("qT"), fence=False)
            qT_ready = (DV, S.val(DV))

        if plmaj < 3:
            continue
        kn_ready_m = {}
        v_ready_m = {}
        knT_ready_m = {}

        def emit_kv(tt):
            rot = tt % 2
            for mg in range(4):
                bank = pC if mg % 2 == 0 else pD
                m0 = D + mg * 512
                mms = []
                for kc in range(KC):
                    mms.append(lambda e, kc=kc, m0=m0, tt=tt, bank=bank:
                               nc.tensor.matmul(
                                   bank[:],
                                   s_xw[:, kc * G + tt * 128: kc * G + (tt + 1) * 128],
                                   wq_sl(kc, m0, 512),
                                   start=(kc == 0), stop=(kc == KC - 1)))
                mm_group(bank, mms, extra_waits=[wq_ld, xw_ld])
                if mg < 2:
                    kraw = scr(mg)
                    bank_read(bank, "vector",
                              lambda e, bank=bank, kraw=kraw: e.tensor_copy(
                                  kraw, bank[:]), fence=False)
                    S.op("scalar", lambda e, kraw=kraw: nc.scalar.activation(
                        scr(2), kraw, AF.Square), waits=[(DV, S.val(DV))], inc=AC,
                        fence=False)
                    S.op("vector", lambda e, kraw=kraw, mg=mg: e.tensor_reduce(
                        k16(0)[:, mg * 8:(mg + 1) * 8],
                        kraw.rearrange("p (h d) -> p h d", d=DK),
                        axis=AX.X, op=ALU.add), inc=DV)
                    S.op("vector", lambda e, mg=mg: e.tensor_reduce(
                        k16(1)[:, mg * 8:(mg + 1) * 8],
                        scr(2).rearrange("p (h d) -> p h d", d=DK),
                        axis=AX.X, op=ALU.add), waits=[(AC, S.val(AC))], inc=DV,
                        fence=False)
                else:
                    if mg == 2:
                        emit_stats(tt, 0)
                    v0 = (mg - 2) * 512
                    bank_read(bank, "vector",
                              lambda e, bank=bank, v0=v0, rot=rot: e.tensor_copy(
                                  s_v[:, rot * D + v0: rot * D + v0 + 512],
                                  bank[:]), extra_waits=rd(("v", rot)),
                              fence=False)
            emit_stats(tt, 1)
            v_ready_m[tt] = (DV, S.val(DV))

        def emit_stats(tt, mg):
            rot = tt % 2
            hs = slice(mg * 8, (mg + 1) * 8)
            S.op("vector", lambda e, hs=hs: e.tensor_scalar(
                k16(2)[:, hs], k16(0)[:, hs], 1.0 / DK, None, op0=ALU.mult),
                inc=DV)
            S.op("vector", lambda e, hs=hs: e.tensor_scalar(
                k16(3)[:, hs], k16(1)[:, hs], 1.0 / DK, None, op0=ALU.mult),
                inc=DV)
            S.op("vector", lambda e, hs=hs: e.tensor_tensor(
                k16(4)[:, hs], k16(2)[:, hs], k16(2)[:, hs], op=ALU.mult),
                inc=DV)
            S.op("vector", lambda e, hs=hs: e.tensor_tensor(
                k16(3)[:, hs], k16(3)[:, hs], k16(4)[:, hs], op=ALU.subtract),
                inc=DV)
            S.op("scalar", lambda e, hs=hs: nc.scalar.activation(
                k16(3)[:, hs], k16(3)[:, hs], AF.Ln, bias=s_eps[:]),
                waits=[(DV, S.val(DV)), prelude_dv], inc=AC)
            S.op("scalar", lambda e, hs=hs: nc.scalar.activation(
                k16(3)[:, hs], k16(3)[:, hs], AF.Exp, scale=-0.5), inc=AC)
            S.op("vector", lambda e, hs=hs: e.tensor_tensor(
                k16(4)[:, hs], k16(2)[:, hs], k16(3)[:, hs], op=ALU.mult),
                waits=[(AC, S.val(AC))], inc=DV)
            S.op("vector", lambda e, hs=hs: e.tensor_scalar(
                k16(4)[:, hs], k16(4)[:, hs], -1.0, None, op0=ALU.mult),
                inc=DV)
            kn_dep = (DV, S.val(DV))
            kraw = scr(mg)
            for hh in range(8):
                h = mg * 8 + hh
                S.op("scalar", lambda e, kraw=kraw, hh=hh, h=h, rot=rot:
                     nc.scalar.activation(
                         s_kn[:, rot * D + h * DK: rot * D + (h + 1) * DK],
                         kraw[:, hh * DK:(hh + 1) * DK], AF.Identity,
                         scale=k16(3)[:, h: h + 1], bias=k16(4)[:, h: h + 1]),
                     waits=[kn_dep] + (rd(("kn", rot)) if hh == 0 and mg == 0
                                       else []),
                     inc=AC, fence=False)
            if mg == 1:
                kn_ready_m[tt] = (AC, S.val(AC))

        def emit_transp(tt):
            rot = tt % 2
            mms = []
            for hp in range(8):
                mms.append(lambda e, hp=hp, rot=rot:
                           nc.tensor.transpose(
                               pG[:, hp * 128:(hp + 1) * 128],
                               s_kn[:, rot * D + hp * 128: rot * D + (hp + 1) * 128],
                               s_ident[:]))
            mm_group(pG, mms, extra_waits=[kn_ready_m[tt], consts_ld])
            bank_read(pG, "vector",
                      lambda e, rot=rot: e.tensor_copy(
                          s_knT[:, rot * KC * 128:(rot + 1) * KC * 128], pG[:]),
                      extra_waits=rd(("knT", rot)), fence=False)
            knT_ready_m[tt] = (DV, S.val(DV))

        def emit_retnet(tt):
            rot = tt % 2
            kn_ready = kn_ready_m[tt]
            knT_ready = knT_ready_m[tt]
            v_ready = v_ready_m[tt]
            if plmaj < 4:
                return
            for h in range(H):
                S.op("scalar", lambda e, h=h, rot=rot: nc.scalar.activation(
                    s_kp[:, h * DK:(h + 1) * DK],
                    s_kn[:, rot * D + h * DK: rot * D + (h + 1) * DK],
                    AF.Copy, scale=s_kpow[:, h: h + 1]),
                    waits=[kn_ready] + (rd("kp") if h == 0 else []), inc=AC,
                    fence=False)
            kp_ready = (AC, S.val(AC))
            if phase_limit < 42:
                return
            par = (g * SUB + tt) % 2
            npar = 1 - par
            sdec_marks = {}

            def emit_st(hp):
                bkA, bkB = (pE, pF) if hp % 2 == 0 else (pA, pB)
                h0, h1 = 2 * hp, 2 * hp + 1
                rot2 = hp % 2
                for li, h, bank in ((0, h0, bkA), (1, h1, bkB)):
                    base = 64 * li
                    mm_group(bank, [lambda e, base=base, hp=hp, rot=rot, tt=tt, bank=bank:
                                    nc.tensor.matmul(
                                        bank[0:128, 0:128],
                                        s_knT[base:base + 64,
                                              rot * KC * 128 + hp * 128:
                                              rot * KC * 128 + (hp + 1) * 128],
                                        s_qR[base:base + 64,
                                             hp * G + tt * 128: hp * G + (tt + 1) * 128],
                                        start=True, stop=True)],
                             extra_waits=[knT_ready, qT_ready])
                for li, bank in ((0, bkA), (1, bkB)):
                    bank_read(bank, "vector",
                              lambda e, bank=bank, hp=hp, li=li, rot2=rot2:
                              e.tensor_tensor(
                                  s_sdec[:, rot2 * 256 + li * 128:
                                         rot2 * 256 + (li + 1) * 128],
                                  bank[:, 0:128],
                                  s_dmat[:, hp * 256 + li * 128:
                                         hp * 256 + (li + 1) * 128],
                                  op=ALU.mult),
                              extra_waits=[prelude_ac], fence=False)
                sdec_marks[hp] = (DV, S.val(DV))

            def emit_rest(hp):
                bkA, bkB = (pE, pF) if hp % 2 == 0 else (pA, pB)
                h0, h1 = 2 * hp, 2 * hp + 1
                rot2 = hp % 2
                sdec_done = sdec_marks[hp]
                if phase_limit < 43:
                    return
                mms = []
                for li, h in ((0, h0), (1, h1)):
                    base = 64 * li
                    mms.append(lambda e, li=li, h=h, base=base, rot=rot, rot2=rot2, bank=bkA:
                               nc.tensor.matmul(
                                   bank[base:base + 64, 256:384],
                                   s_v[:, rot * D + h * DK: rot * D + (h + 1) * DK],
                                   s_sdec[:, rot2 * 256 + li * 128:
                                          rot2 * 256 + (li + 1) * 128],
                                   start=True, stop=False))
                    mms.append(lambda e, li=li, base=base, hp=hp, par=par, bank=bkA, tt=tt:
                               nc.tensor.matmul(
                                   bank[base:base + 64, 256:384],
                                   sbf_sl(par, hp)[base:base + 64, :],
                                   s_qT[base:base + 64,
                                        hp * G + tt * 128: hp * G + (tt + 1) * 128],
                                   start=False, stop=True))
                    if phase_limit >= 44:
                        mms.append(lambda e, li=li, h=h, base=base, rot=rot, bank=bkA:
                                   nc.tensor.matmul(
                                       bank[base:base + 64, 384:448],
                                       s_kp[:, h * DK:(h + 1) * DK],
                                       s_v[:, rot * D + h * DK: rot * D + (h + 1) * DK],
                                       start=True, stop=True))
                extra = [sdec_done, v_ready, kp_ready]
                if state_zeroed is not None:
                    extra.append(state_zeroed)
                mm_group(bkA, mms, extra_waits=extra)
                bank_read(bkA, "vector",
                          lambda e, bank=bkA, hp=hp, tt=tt: e.tensor_copy(
                              s_yT[:, hp * G + tt * 128: hp * G + (tt + 1) * 128],
                              bank[:, 256:384]),
                          extra_waits=rd("yT") if (hp == 0 and tt == 0) else [],
                          fence=False)
                if phase_limit < 44:
                    return
                bank_read(bkA, "vector",
                          lambda e, bank=bkA, hp=hp, par=par, npar=npar:
                          e.scalar_tensor_tensor(
                              sf_sl(npar, hp), sf_sl(par, hp),
                              s_gCp[:, hp: hp + 1], bank[:, 384:448],
                              op0=ALU.mult, op1=ALU.add),
                          extra_waits=[pair_done], fence=False)
                S.op("scalar", lambda e, hp=hp, npar=npar: nc.scalar.activation(
                    sbf_sl(npar, hp), sf_sl(npar, hp), AF.Copy),
                    waits=[(DV, S.val(DV))], inc=AC, fence=False)

            emit_st(0)
            for hp in range(8):
                if hp + 1 < 8:
                    emit_st(hp + 1)
                emit_rest(hp)
            note_read(("kn", rot), AC)
            note_read(("v", rot), PE)
            note_read(("knT", rot), PE)
            note_read("kp", PE)

        emit_kv(0)
        emit_qpass()
        for tt in range(TT):
            emit_transp(tt)
            if tt + 1 < TT:
                emit_kv(tt + 1)
            emit_retnet(tt)
        note_read("state", AC)
        note_read("big", PE)
        yT_ready = (DV, S.val(DV))
        state_bf_ready = (AC, S.val(AC))

        if plmaj < 5:
            continue
        if bname(pC) in bank_free:
            S.op("tensor", lambda e: e.nop(), waits=[bank_free[bname(pC)],
                                                     bank_free[bname(pD)]])
        for mc in range(KC):
            bank = pA if mc % 2 == 0 else pB
            mms = []
            for kc in range(KC):
                mms.append(lambda e, kc=kc, mc=mc, bank=bank:
                           nc.tensor.matmul(bank[:], wo_sl(kc, mc * 128, 128),
                                            s_yT[:, kc * G:(kc + 1) * G],
                                            start=(kc == 0), stop=(kc == KC - 1)))
            mm_group(bank, mms, extra_waits=[wo_ld, yT_ready])
            bank_read(bank, "vector",
                      lambda e, mc=mc, bank=bank: e.tensor_tensor(
                          s_xres[:, mc * G:(mc + 1) * G], bank[:],
                          s_xw[:, mc * G:(mc + 1) * G], op=ALU.add),
                      extra_waits=rd("xres") if mc == 0 else [], fence=False)
            xres_mc = (DV, S.val(DV))
            S.op("scalar", lambda e, mc=mc: nc.scalar.activation(
                s_act[:, mc * G:(mc + 1) * G], s_xres[:, mc * G:(mc + 1) * G],
                AF.Square),
                waits=[xres_mc] + (rd("act") if mc == 0 else []), inc=AC,
                fence=False)
            sq_mc = (AC, S.val(AC))
            S.op("tensor", lambda e, mc=mc: nc.tensor.matmul(
                pC[0:1, 0:G], s_onescol[:], s_xres[:, mc * G:(mc + 1) * G],
                start=(mc == 0), stop=(mc == KC - 1)),
                waits=[xres_mc] + ([consts_ld] if mc == 0 else []),
                inc=PE if mc == KC - 1 else None)
            S.op("tensor", lambda e, mc=mc: nc.tensor.matmul(
                pD[0:1, 0:G], s_onescol[:], s_act[:, mc * G:(mc + 1) * G],
                start=(mc == 0), stop=(mc == KC - 1)),
                waits=[sq_mc], inc=PE if mc == KC - 1 else None)
        bank_full[bname(pC)] = (PE, S.val(PE))
        bank_full[bname(pD)] = (PE, S.val(PE))
        note_read("xw", DV)
        note_read("r2", PE)
        xres_ready = (DV, S.val(DV))
        note_read("act", PE)
        bank_read(pC, "vector", lambda e: e.tensor_scalar(
            lnr(0), pC[0:1, 0:G], 1.0 / D, None, op0=ALU.mult),
            fence=False)
        bank_read(pD, "vector", lambda e: e.tensor_scalar(
            lnr(1), pD[0:1, 0:G], 1.0 / D, None, op0=ALU.mult),
            fence=False)
        S.op("vector", lambda e: e.tensor_tensor(
            lnr(2), lnr(0), lnr(0), op=ALU.mult), inc=DV)
        S.op("vector", lambda e: e.tensor_tensor(
            lnr(1), lnr(1), lnr(2), op=ALU.subtract), inc=DV)
        S.op("scalar", lambda e: nc.scalar.activation(
            lnr(1), lnr(1), AF.Ln, bias=s_eps[0:1, :]),
            waits=[(DV, S.val(DV))], inc=AC)
        S.op("scalar", lambda e: nc.scalar.activation(
            lnr(1), lnr(1), AF.Exp, scale=-0.5), inc=AC)
        S.op("vector", lambda e: e.tensor_copy(lnb(0), lnr(1)),
             waits=[(AC, S.val(AC))], inc=DV)
        S.op("vector", lambda e: e.scalar_tensor_tensor(
            lnr(2), lnr(0), -1.0, lnr(1), op0=ALU.mult, op1=ALU.mult), inc=DV)
        S.op("vector", lambda e: e.tensor_copy(lnb(1), lnr(2)), inc=DV)
        ab_ready = (DV, S.val(DV))
        mm_group(pC, [lambda e: nc.tensor.matmul(
            pC[:, 0:G], s_ones1[:], lnb(0), start=True, stop=True)],
            extra_waits=[ab_ready])
        mm_group(pD, [lambda e: nc.tensor.matmul(
            pD[:, 0:G], s_ones1[:], lnb(1), start=True, stop=True)],
            extra_waits=[ab_ready])
        x1_kc_marks = []
        for kc in range(KC):
            S.op("vector", lambda e, kc=kc: e.tensor_tensor(
                scr(kc % 2), s_xres[:, kc * G:(kc + 1) * G], pC[:, 0:G],
                op=ALU.mult),
                waits=([bank_full[bname(pC)], bank_full[bname(pD)]]
                       if kc == 0 else ()), inc=DV, fence=False)
            if kc >= 1:
                S.op("vector", lambda e, kc=kc: e.tensor_tensor(
                    s_x1[:, (kc - 1) * G: kc * G], scr((kc - 1) % 2), pD[:, 0:G],
                    op=ALU.add),
                    waits=rd("x1") if kc == 1 else (), inc=DV, fence=False)
                x1_kc_marks.append((DV, S.val(DV)))
        S.op("vector", lambda e: e.tensor_tensor(
            s_x1[:, (KC - 1) * G: KC * G], scr((KC - 1) % 2), pD[:, 0:G],
            op=ALU.add), inc=DV)
        x1_kc_marks.append((DV, S.val(DV)))
        x1_ready = (DV, S.val(DV))
        bank_free[bname(pC)] = (DV, S.val(DV))
        bank_free[bname(pD)] = (DV, S.val(DV))


        if plmaj < 6:
            continue
        for tt in range(TT):
            mms = []
            for kc in range(KC):
                mms.append(lambda e, kc=kc, tt=tt: nc.tensor.matmul(
                    pE[:, 0:16],
                    s_x1[:, kc * G + tt * 128: kc * G + (tt + 1) * 128],
                    s_rt[:, kc * E:(kc + 1) * E],
                    start=(kc == 0), stop=(kc == KC - 1)))
            mm_group(pE, mms, extra_waits=[x1_ready, pre_pd])
            bank_read(pE, "vector", lambda e: e.tensor_tensor(
                s_lg[:, 0:16], pE[:, 0:16], s_brep[:], op=ALU.add),
                extra_waits=[consts_ld], fence=False)
            S.op("vector", lambda e: e.tensor_reduce(
                sm(0), s_lg[:, 0:16], axis=AX.X, op=ALU.max, negate=True), inc=DV)
            S.op("scalar", lambda e: nc.scalar.activation(
                s_lg[:, 16:32], s_lg[:, 0:16], AF.Exp, bias=sm(0),
                accum_out=sm(1)), waits=[(DV, S.val(DV))], inc=AC)
            S.op("vector", lambda e: e.reciprocal(sm(2), sm(1)),
                 waits=[(AC, S.val(AC))], inc=DV, fence=False)
            S.op("vector", lambda e: e.tensor_scalar(
                s_lg[:, 32:48], s_lg[:, 16:32], sm(2), None, op0=ALU.mult), inc=DV)
            S.op("vector", lambda e: e.max(out=s_top8[:], in_=s_lg[:, 32:48]), inc=DV)
            S.op("vector", lambda e, tt=tt: e.scalar_tensor_tensor(
                s_w[:, tt * E:(tt + 1) * E], s_lg[:, 32:48], s_top8[:, 3:4],
                s_lg[:, 32:48], op0=ALU.is_ge, op1=ALU.mult),
                waits=rd("w") if tt == 0 else [], inc=DV)
            S.op("vector", lambda e, tt=tt: e.tensor_copy(
                s_wbf[:], s_w[:, tt * E:(tt + 1) * E]), inc=DV)
            w_ready = (DV, S.val(DV))
            mm_group(pG, [lambda e: nc.tensor.transpose(
                pG[0:16, 0:128], s_wbf[:, 0:16], s_ident[:])],
                extra_waits=[w_ready])
            bank_read(pG, "vector", lambda e, tt=tt: e.tensor_copy(
                s_wT[0:16, tt * 128:(tt + 1) * 128], pG[0:16, 0:128]),
                extra_waits=rd("wT") if tt == 0 else [], fence=False)
            wT_ready = (DV, S.val(DV))
            for hh in range(2):
                bank = pE if hh == 0 else pF
                mm_group(bank, [lambda e, tt=tt, hh=hh, bank=bank: nc.tensor.matmul(
                    bank[:, 0:512], s_wT[0:16, tt * 128:(tt + 1) * 128],
                    s_db[0:16, hh * 512:(hh + 1) * 512], start=True, stop=True)],
                    extra_waits=[wT_ready, pre_pd])
                bank_read(bank, "vector", lambda e, tt=tt, hh=hh, bank=bank:
                          e.tensor_copy(
                              s_acc[:, tt * D + hh * 512: tt * D + (hh + 1) * 512],
                              bank[:, 0:512]),
                          extra_waits=rd("acc") if (tt == 0 and hh == 0) else [],
                          fence=False)
        acc_init = (DV, S.val(DV))
        w_all = (DV, S.val(DV))

        for tt in range(TT):
            mms = []
            for kc in range(KC):
                mms.append(lambda e, kc=kc, tt=tt: nc.tensor.transpose(
                    pG[:, kc * 128:(kc + 1) * 128],
                    s_x1[:, kc * G + tt * 128: kc * G + (tt + 1) * 128],
                    s_ident[:]))
            mm_group(pG, mms, extra_waits=[x1_kc_marks[KC - 1], consts_ld])
            bank_read(pG, "vector",
                      lambda e, tt=tt: e.tensor_copy(x1tok_sl(tt), pG[:]),
                      fence=False)
        x1tok_ready = (DV, S.val(DV))


        if plmaj < 7:
            continue
        scr_reader = {}
        for ex in range(E):
            if first:
                S.op("gpsimd", lambda e, ex=ex: e.dma_start(
                    out=s_big[:, 0:KC * GU].rearrange("p (kc n) -> p kc n", kc=KC),
                    in_=gu_proj[ex].rearrange("(kc p) n -> p kc n", p=128)),
                    waits=rd("big") + [wq_wb], inc=PD)
                gu_ld = (PD, S.val(PD))
                S.op("gpsimd", lambda e, ex=ex: e.dma_start(
                    out=gu_cache[ex].rearrange("(kc p) n -> p kc n", p=128),
                    in_=s_big[:, 0:KC * GU].rearrange("p (kc n) -> p kc n", kc=KC)),
                    waits=[gu_ld], inc=PD)
                S.op("gpsimd", lambda e, ex=ex: e.dma_start(
                    out=s_r2[:, 0:KC * D].rearrange("p (kc n) -> p kc n", kc=KC),
                    in_=down_proj[ex].rearrange("(kc p) n -> p kc n", p=128)),
                    waits=rd("r2") + [wo_wb], inc=PD)
                dn_ld = (PD, S.val(PD))
                S.op("gpsimd", lambda e, ex=ex: e.dma_start(
                    out=down_cache[ex].rearrange("(kc p) n -> p kc n", p=128),
                    in_=s_r2[:, 0:KC * D].rearrange("p (kc n) -> p kc n", kc=KC)),
                    waits=[dn_ld], inc=PD)
                if ex == E - 1:
                    gu_wb = (PD, S.val(PD))
            else:
                S.op("sync", lambda e, ex=ex: e.dma_start(
                    out=s_big[:, 0:KC * GU].rearrange("p (kc n) -> p kc n", kc=KC),
                    in_=gu_cache[ex].rearrange("(kc p) n -> p kc n", p=128)),
                    waits=rd("big") + [gu_wb], inc=SD)
                gu_ld = (SD, S.val(SD))
                S.op("sync", lambda e, ex=ex: e.dma_start(
                    out=s_r2[:, 0:KC * D].rearrange("p (kc n) -> p kc n", kc=KC),
                    in_=down_cache[ex].rearrange("(kc p) n -> p kc n", p=128)),
                    waits=rd("r2") + [gu_wb], inc=SD)
                dn_ld = (SD, S.val(SD))

            for j in range(KC):
                gbank = pA if j % 2 == 0 else pB
                ubank = pC if j % 2 == 0 else pD
                mms = []
                for kc in range(KC):
                    mms.append(lambda e, kc=kc, j=j, gbank=gbank: nc.tensor.matmul(
                        gbank[:], gu_sl(kc, j * 128, 128), s_x1[:, kc * G:(kc + 1) * G],
                        start=(kc == 0), stop=(kc == KC - 1)))
                mm_group(gbank, mms, extra_waits=[gu_ld, x1_ready])
                mms = []
                for kc in range(KC):
                    mms.append(lambda e, kc=kc, j=j, ubank=ubank: nc.tensor.matmul(
                        ubank[:], gu_sl(kc, NI + j * 128, 128),
                        s_x1[:, kc * G:(kc + 1) * G],
                        start=(kc == 0), stop=(kc == KC - 1)))
                mm_group(ubank, mms, extra_waits=[gu_ld, x1_ready])
                svs = scr(0 + (j % 2))
                bank_read(gbank, "scalar",
                          lambda e, gbank=gbank, ex=ex, j=j, svs=svs:
                          nc.scalar.activation(
                              svs, gbank[:], AF.Silu, scale=ALPHA,
                              bias=s_gubsc[:, ex * 8 + j: ex * 8 + j + 1]),
                          extra_waits=[prelude_dv] +
                          scr_reader.get(("sv", j % 2), []), fence=False)
                sv_done = (AC, S.val(AC))
                S.op("vector", lambda e, svs=svs: e.tensor_scalar(
                    svs, svs, SV_CLIP, 1.0 / ALPHA, op0=ALU.min, op1=ALU.mult),
                    waits=[sv_done], inc=DV, fence=False)
                u2 = scr(2 + (j % 2))
                bank_read(ubank, "vector",
                          lambda e, ubank=ubank, ex=ex, j=j, u2=u2: e.tensor_scalar(
                              u2, ubank[:],
                              s_gubu1[:, ex * 8 + j: ex * 8 + j + 1],
                              LIMIT + 1.0, op0=ALU.add, op1=ALU.min),
                          extra_waits=[prelude_dv] +
                          scr_reader.get(("u", j % 2), []), fence=False)
                S.op("vector", lambda e, j=j, u2=u2, svs=svs: e.scalar_tensor_tensor(
                    s_act[:, j * G:(j + 1) * G], u2, -(LIMIT - 1.0), svs,
                    op0=ALU.max, op1=ALU.mult),
                    waits=(rd("act") if (j == 0 and ex == 0) else []), inc=DV)
                scr_reader[("sv", j % 2)] = [(DV, S.val(DV))]
                scr_reader[("u", j % 2)] = [(DV, S.val(DV))]
            act_ready = (DV, S.val(DV))
            note_read("big", PE)

            for tt in range(TT):
                for hh in range(2):
                    bank = pE if (tt * 2 + hh) % 2 == 0 else pF
                    mms = []
                    for kc in range(KC):
                        mms.append(lambda e, kc=kc, tt=tt, hh=hh, bank=bank:
                                   nc.tensor.matmul(
                                       bank[:],
                                       s_act[:, kc * G + tt * 128: kc * G + (tt + 1) * 128],
                                       dn_sl(kc, hh * 512, 512),
                                       start=(kc == 0), stop=(kc == KC - 1)))
                    mm_group(bank, mms, extra_waits=[act_ready, dn_ld, acc_init])
                    bank_read(bank, "vector",
                              lambda e, tt=tt, hh=hh, ex=ex, bank=bank:
                              e.scalar_tensor_tensor(
                                  s_acc[:, tt * D + hh * 512: tt * D + (hh + 1) * 512],
                                  bank[:], s_w[:, tt * E + ex: tt * E + ex + 1],
                                  s_acc[:, tt * D + hh * 512: tt * D + (hh + 1) * 512],
                                  op0=ALU.mult, op1=ALU.add),
                              extra_waits=[w_all], fence=False)
            note_read("r2", PE)
        acc_ready = (DV, S.val(DV))
        note_read("w", DV)
        note_read("wT", PE)

        if plmaj < 8:
            continue
        for tt in range(TT):
            rot = tt % 2
            S.op("vector", lambda e, tt=tt, rot=rot: e.tensor_tensor(
                s_r[:, rot * D:(rot + 1) * D], s_acc[:, tt * D:(tt + 1) * D],
                x1tok_sl(tt), op=ALU.add),
                waits=[acc_ready, x1tok_ready] + rd(("r", rot)), inc=DV,
                fence=False)
            S.op("vector", lambda e, rot=rot: e.tensor_reduce(
                sm(3), s_r[:, rot * D:(rot + 1) * D], axis=AX.X, op=ALU.add,
                negate=True), inc=DV)
            S.op("vector", lambda e: e.tensor_scalar(
                sm(3), sm(3), 1.0 / D, None, op0=ALU.mult), inc=DV)
            r_done = (DV, S.val(DV))
            S.op("scalar", lambda e, rot=rot: nc.scalar.activation(
                s_r[:, (1 - rot) * D:(2 - rot) * D], s_r[:, rot * D:(rot + 1) * D],
                AF.Square, bias=sm(3), accum_out=sm(4)), waits=[r_done], inc=AC,
                fence=False)
            S.op("scalar", lambda e: nc.scalar.activation(
                sm(5), sm(4), AF.Ln, scale=1.0 / D, bias=s_eps[:]), inc=AC)
            S.op("scalar", lambda e: nc.scalar.activation(
                sm(5), sm(5), AF.Exp, scale=-0.5), inc=AC)
            S.op("vector", lambda e, tt=tt, rot=rot: e.tensor_scalar(
                s_stage[:, rot * D:(rot + 1) * D], s_r[:, rot * D:(rot + 1) * D],
                sm(3), sm(5), op0=ALU.add, op1=ALU.mult),
                waits=[(AC, S.val(AC))] + rd(("stage", rot)), inc=DV)
            note_read(("r", rot), DV)
            S.op("sync", lambda e, b=b, t0=t0, tt=tt, rot=rot: e.dma_start(
                out=out[b, t0 + tt * 128: t0 + (tt + 1) * 128, :],
                in_=s_stage[:, rot * D:(rot + 1) * D]),
                waits=[(DV, S.val(DV))], inc=SD)
            note_read(("stage", rot), SD)
        note_read("acc", DV)
        note_read("xres", DV)
        note_read("x1", PE)
        note_read("qT", PE)
        note_read("qR", PE)
        note_read("yT", PE)

    if dump:
        dump_list = [s_xw, s_qR, s_qT, s_kn, s_knT, s_v, s_yT, s_xres, s_x1,
                     s_w, s_acc, s_sf, s_sbf, s_sdec, s_kp, s_stage, s_r,
                     s_llg, s_gpow, s_dmat, s_kpow, s_gCp, s_lgbc, s_scr,
                     s_act]
        allw = [(PE, S.val(PE)), (DV, S.val(DV)), (AC, S.val(AC)),
                (SD, S.val(SD)), (PD, S.val(PD))]
        for t in dump_list:
            nm = t.name if isinstance(t.name, str) else t.name()
            dt = nc.dram_tensor("dbg_" + nm, list(t.shape), t.dtype,
                                kind="ExternalOutput").ap()
            S.op("sync", lambda e, t=t, dt=dt: e.dma_start(out=dt, in_=t[:]),
                 waits=allw, inc=SD)

    S.op("sync", lambda e: e.nop(), waits=[(SD, S.val(SD))])

    with nc.Block() as block:
        S.emit_all(block)
    return nc


def _consts():
    i_idx = np.arange(128, dtype=np.float32)
    idxT = np.maximum(i_idx[None, :] - i_idx[:, None], 0.0).astype(np.float32)
    maskT = np.where(i_idx[None, :] >= i_idx[:, None], 0.0, NEG_BIG).astype(np.float32)
    e1 = np.broadcast_to(i_idx[None, :] + 1.0, (128, 128)).astype(np.float32).copy()
    c2 = (127.0 - i_idx)[:, None].astype(np.float32).copy()
    ident = np.eye(128, dtype=np.float32).astype(bfdt)
    ones1 = np.ones((1, 128), dtype=np.float32).astype(bfdt)
    onescol = np.ones((128, 1), dtype=np.float32).astype(bfdt)
    return dict(c_idxT=idxT, c_maskT=maskT, c_e1=e1, c_c2=c2, c_ident=ident,
                c_ones1=ones1, c_onescol=onescol)


def kernel(L0_attn_x, L0_attn_W_qkv, L0_attn_W_o, L0_attn_log_gamma,
           L0_ffn_router_weight, L0_ffn_router_bias, L0_ffn_gate_up_proj,
           L0_ffn_gate_up_bias, L0_ffn_down_proj, L0_ffn_down_bias):
    B, L, _ = L0_attn_x.shape
    ncores = 8
    nseq = B // ncores
    nc = build_nc(NSEQ=nseq, L=L, G=512)

    xt = np.ascontiguousarray(np.transpose(np.asarray(L0_attn_x), (0, 2, 1)))
    wqkv_t = np.ascontiguousarray(np.asarray(L0_attn_W_qkv).T)
    wo_t = np.ascontiguousarray(np.asarray(L0_attn_W_o).T)
    router_t = np.ascontiguousarray(np.asarray(L0_ffn_router_weight).T)
    gub_t = np.ascontiguousarray(
        np.asarray(L0_ffn_gate_up_bias).reshape(16, 16, 128).transpose(2, 0, 1)
        .reshape(128, 256))
    consts = _consts()

    base = dict(wqkv_t=wqkv_t, wo_t=wo_t, router_t=router_t,
                log_gamma=np.asarray(L0_attn_log_gamma),
                r_bias=np.asarray(L0_ffn_router_bias), gub_t=gub_t,
                gu_proj=np.asarray(L0_ffn_gate_up_proj),
                down_proj=np.asarray(L0_ffn_down_proj),
                down_bias=np.asarray(L0_ffn_down_bias), **consts)
    in_maps = []
    for c in range(ncores):
        m = dict(base)
        m["x_t"] = np.ascontiguousarray(xt[c * nseq:(c + 1) * nseq])
        in_maps.append(m)

    res = run_bass_kernel_spmd(nc, in_maps, core_ids=list(range(ncores)))
    return np.concatenate([r["out"] for r in res.results], axis=0)


# version 17
# speedup vs baseline: 1.2332x; 1.0800x over previous
import numpy as np
import ml_dtypes
import concourse.bass as bass
import concourse.mybir as mybir
from concourse.bass_utils import run_bass_kernel_spmd

F32 = mybir.dt.float32
BF16 = mybir.dt.bfloat16
AF = mybir.ActivationFunctionType
ALU = mybir.AluOpType
AX = mybir.AxisListType
bfdt = ml_dtypes.bfloat16

D = 1024
H = 16
DK = 64
E = 16
NI = 1024
GU = 2 * NI
C = 128
KC = D // 128
LN_EPS = 1e-5
ALPHA = 1.702
LIMIT = 7.0
NEG_BIG = -1.0e9


class Sched:

    def __init__(self, nc):
        self.nc = nc
        self.ops = []
        self.cnt = {}
        self.sems = {}

    def sem(self, name):
        self.sems[name] = self.nc.alloc_semaphore(name=name)
        self.cnt[name] = 0
        return name

    def op(self, eng, fn, waits=(), inc=None, n=1, fence=True):
        if inc in ("SD", "PD"):
            n = 16
        waits = tuple((s, v) for (s, v) in waits if v > 0)
        for s, v in waits:
            assert v <= self.cnt[s], f"future wait {s} {v} > {self.cnt[s]}"
        if inc is not None:
            self.cnt[inc] += n
        self.ops.append((eng, waits, fn, inc, n, fence))

    def val(self, sem):
        return self.cnt[sem]

    def emit_all(self, block):
        engmap = {
            "tensor": block.tensor, "vector": block.vector,
            "scalar": block.scalar, "sync": block.sync, "gpsimd": block.gpsimd,
        }
        selfsem = {"vector": "DV", "scalar": "AC", "gpsimd": "PL"}
        for name, deco in engmap.items():
            ops = [o for o in self.ops if o[0] == name]

            def make(ops_, name_):
                def body(eng):
                    seen = {}
                    own = selfsem.get(name_)
                    cnt = 0
                    for (_e, waits, fn, inc, n, fence) in ops_:
                        if own is not None and cnt > 0 and fence:
                            if seen.get(own, 0) < cnt:
                                eng.wait_ge(self.sems[own], cnt)
                                seen[own] = cnt
                        for (s, v) in waits:
                            if seen.get(s, 0) < v:
                                eng.wait_ge(self.sems[s], v)
                                seen[s] = v
                        inst = fn(eng)
                        if inc is not None:
                            inst.then_inc(self.sems[inc], n)
                            if inc == own:
                                cnt += n
                return body
            deco(make(ops, name))


def build_nc(NSEQ=2, L=4096, G=512, n_dbg=0, phase_limit=99, dump=False):
    nc = bass.Bass("TRN2", target_bir_lowering=False,
                   detect_race_conditions=False)
    plmaj = 4 if 40 <= phase_limit < 90 else phase_limit
    NCH = (L // G) * NSEQ
    TT = G // 128
    SUB = G // C

    def din(name, shape, dt=F32):
        return nc.dram_tensor(name, list(shape), dt, kind="ExternalInput").ap()

    x_t = din("x_t", [NSEQ, D, L])
    wqkv_t = din("wqkv_t", [D, 3 * D])
    wo_t = din("wo_t", [D, D])
    router_t = din("router_t", [D, E])
    log_gamma = din("log_gamma", [H])
    r_bias = din("r_bias", [E])
    gub_t = din("gub_t", [128, 256])
    gu_proj = din("gu_proj", [E, D, GU])
    down_proj = din("down_proj", [E, NI, D])
    down_bias = din("down_bias", [E, D])
    c_ident = din("c_ident", [128, 128], BF16)
    c_idxT = din("c_idxT", [128, 128])
    c_maskT = din("c_maskT", [128, 128])
    c_e1 = din("c_e1", [128, 128])
    c_c2 = din("c_c2", [128, 1])
    c_ones1 = din("c_ones1", [1, 128], BF16)
    c_onescol = din("c_onescol", [128, 1], BF16)

    out = nc.dram_tensor("out", [NSEQ, L, D], F32, kind="ExternalOutput").ap()
    dbg = None
    if n_dbg:
        dbg = nc.dram_tensor("dbg", [n_dbg, 128, 4096], F32,
                             kind="ExternalOutput").ap()

    wqkv_cache = nc.dram_tensor("wqkv_cache", [D, 3 * D], BF16, kind="Internal").ap()
    wo_cache = nc.dram_tensor("wo_cache", [D, D], BF16, kind="Internal").ap()
    gu_cache = nc.dram_tensor("gu_cache", [E, D, GU], BF16, kind="Internal").ap()
    down_cache = nc.dram_tensor("down_cache", [E, NI, D], BF16, kind="Internal").ap()

    A = nc.alloc_sbuf_tensor
    s_big = A("s_big", [128, KC * 3 * D], BF16)
    wq_sl = lambda kc, m0, w: s_big[:, kc * 3 * D + m0: kc * 3 * D + m0 + w]
    gu_sl = lambda kc, n0, w: s_big[:, kc * GU + n0: kc * GU + n0 + w]
    s_r2 = A("s_r2", [128, KC * D], BF16)
    wo_sl = lambda kc, m0, w: s_r2[:, kc * D + m0: kc * D + m0 + w]
    dn_sl = lambda kc, h0, w: s_r2[:, kc * D + h0: kc * D + h0 + w]

    s_xw = A("s_xw", [128, KC * G], BF16)
    s_qT = A("s_qT", [128, KC * G], BF16)
    s_qR = A("s_qR", [128, KC * G], BF16)
    s_yT = A("s_yT", [128, KC * G], BF16)
    s_xres = A("s_xres", [128, KC * G], BF16)
    x1tok_sl = lambda tt: s_xres[:, tt * D:(tt + 1) * D]
    s_x1 = A("s_x1", [128, KC * G], BF16)
    s_act = A("s_act", [128, KC * G], BF16)
    s_kn = A("s_kn", [128, 2 * D], BF16)
    s_knT = A("s_knT", [128, 2 * KC * 128], BF16)
    s_v = A("s_v", [128, 2 * D], BF16)
    s_kp = A("s_kp", [128, D], BF16)
    s_sdec = A("s_sdec", [128, 2 * 256], BF16)
    s_acc = A("s_acc", [128, TT * D], F32)
    s_stage = A("s_stage", [128, 2 * D], F32)
    s_r = A("s_r", [128, 2 * D], F32)
    s_scr = A("s_scr", [128, 4 * 512], F32)
    s_eps = A("s_eps", [128, 1], F32)
    scr = lambda i, w=512: s_scr[:, i * 512: i * 512 + w]
    s_sf = A("s_sf", [128, 2 * 8 * DK], F32)
    s_sbf = A("s_sbf", [128, 2 * 8 * DK], BF16)
    sf_sl = lambda par, hp: s_sf[:, par * 512 + hp * DK: par * 512 + (hp + 1) * DK]
    sbf_sl = lambda par, hp: s_sbf[:, par * 512 + hp * DK: par * 512 + (hp + 1) * DK]

    s_ident = A("s_ident", [128, 128], BF16)
    s_idxT = A("s_idxT", [128, 128], F32)
    s_maskT = A("s_maskT", [128, 128], F32)
    s_e1 = A("s_e1", [128, 128], F32)
    s_c2 = A("s_c2", [128, 1], F32)
    s_ones1 = A("s_ones1", [1, 128], BF16)
    s_onescol = A("s_onescol", [128, 1], BF16)
    s_dmat = A("s_dmat", [128, 8 * 256], BF16)
    s_gpow = A("s_gpow", [128, 8 * 128], BF16)
    s_lgbc = A("s_lgbc", [128, H], F32)
    s_brep = A("s_brep", [128, E], F32)
    s_llg = A("s_llg", [128, H], F32)
    s_gC = A("s_gC", [128, H], F32)
    s_gCp = A("s_gCp", [128, 8], F32)
    s_llgp = A("s_llgp", [128, 8], F32)
    s_kpow = A("s_kpow", [128, H], F32)
    s_gub = A("s_gub", [128, 256], F32)
    s_gubsc = A("s_gubsc", [128, 128], F32)
    s_gubu1 = A("s_gubu1", [128, 128], F32)
    s_rt = A("s_rt", [128, KC * E], BF16)
    s_db = A("s_db", [16, D], BF16)
    s_lg = A("s_lg", [128, 16 * 8], F32)
    s_w = A("s_w", [128, TT * E], F32)
    s_wbf = A("s_wbf", [128, E], BF16)
    s_wT = A("s_wT", [16, TT * 128], BF16)
    s_top8 = A("s_top8", [128, 8], F32)
    s_sm = A("s_sm", [128, 64], F32)
    sm = lambda i, w=1: s_sm[:, i: i + w]
    s_ln = A("s_ln", [1, 3 * 512], F32)
    lnr = lambda i: s_ln[0:1, i * 512: (i + 1) * 512]
    s_lnb = A("s_lnb", [1, 2 * 512], BF16)
    lnb = lambda i: s_lnb[0:1, i * 512: (i + 1) * 512]
    s_k16 = A("s_k16", [128, 6 * 16], F32)
    k16 = lambda i: s_k16[:, i * 16: (i + 1) * 16]

    P = nc.alloc_psum_tensor
    pA = P("pA", [128, 512], F32)
    pB = P("pB", [128, 512], F32)
    pC = P("pC", [128, 512], F32)
    pD = P("pD", [128, 512], F32)
    pE = P("pE", [128, 512], F32)
    pF = P("pF", [128, 512], F32)
    pG = P("pG", [128, 1024], BF16)

    S = Sched(nc)
    PE = S.sem("PE"); DV = S.sem("DV"); AC = S.sem("AC")
    SD = S.sem("SD"); PD = S.sem("PD"); PL = S.sem("PL")

    bank_free = {}
    bank_full = {}

    def bname(bank):
        return id(bank)

    def mm_group(bank, mms, extra_waits=()):
        name = bname(bank)
        w = list(extra_waits)
        if name in bank_free:
            w.append(bank_free[name])
        for i, fn in enumerate(mms):
            S.op("tensor", fn, waits=tuple(w) if i == 0 else (),
                 inc=PE if i == len(mms) - 1 else None)
        bank_full[name] = (PE, S.val(PE))

    def bank_read(bank, eng, fn, extra_waits=(), inc=None, fence=True):
        name = bname(bank)
        w = list(extra_waits)
        if name in bank_full:
            w.append(bank_full[name])
        semmap = {"vector": DV, "scalar": AC, "gpsimd": PL}
        s = semmap[eng] if inc is None else inc
        S.op(eng, fn, waits=tuple(w), inc=s, fence=fence)
        bank_free[name] = (s, S.val(s))

    last_read = {}

    def note_read(key, sem):
        last_read[key] = (sem, S.val(sem))

    def rd(key):
        return [last_read[key]] if key in last_read else []

    for dst, src in [(s_ident, c_ident), (s_idxT, c_idxT), (s_maskT, c_maskT),
                     (s_e1, c_e1), (s_c2, c_c2), (s_ones1, c_ones1),
                     (s_onescol, c_onescol), (s_gub, gub_t)]:
        S.op("sync", lambda e, d=dst, s=src: e.dma_start(out=d[:], in_=s),
             inc=SD)
    S.op("sync", lambda e: e.dma_start(
        out=s_lgbc[:], in_=log_gamma.unsqueeze(0).to_broadcast([128, H])), inc=SD)
    S.op("sync", lambda e: e.dma_start(
        out=s_brep[:], in_=r_bias.unsqueeze(0).to_broadcast([128, E])), inc=SD)
    consts_ld = (SD, S.val(SD))
    S.op("gpsimd", lambda e: e.dma_start(
        out=s_rt[:].rearrange("p (kc e) -> p kc e", kc=KC),
        in_=router_t.rearrange("(kc p) e -> p kc e", p=128)), inc=PD)
    S.op("gpsimd", lambda e: e.dma_start(out=s_db[:], in_=down_bias), inc=PD)
    pre_pd = (PD, S.val(PD))

    S.op("scalar", lambda e: nc.scalar.activation(
        s_llg[:], s_lgbc[:], AF.Sigmoid), waits=[consts_ld], inc=AC)
    S.op("scalar", lambda e: nc.scalar.activation(
        s_llg[:], s_llg[:], AF.Ln), inc=AC)
    S.op("scalar", lambda e: nc.scalar.activation(
        s_gC[:], s_llg[:], AF.Exp, scale=float(C)), inc=AC)
    llg_done = (AC, S.val(AC))
    S.op("vector", lambda e: e.tensor_scalar(
        s_kpow[:], s_llg[:], s_c2[:], None, op0=ALU.mult),
        waits=[llg_done], inc=DV)
    S.op("scalar", lambda e: nc.scalar.activation(
        s_kpow[:], s_kpow[:], AF.Exp), waits=[(DV, S.val(DV))], inc=AC)
    S.op("vector", lambda e: e.tensor_copy(
        s_llgp[0:64, :], s_llg[0:64, :].rearrange("p (a two) -> p a two", two=2)[:, :, 0]),
        waits=[llg_done], inc=DV)
    S.op("vector", lambda e: e.tensor_copy(
        s_llgp[64:128, :], s_llg[64:128, :].rearrange("p (a two) -> p a two", two=2)[:, :, 1]),
        inc=DV)
    S.op("vector", lambda e: e.tensor_copy(
        s_gCp[0:64, :], s_gC[0:64, :].rearrange("p (a two) -> p a two", two=2)[:, :, 0]),
        inc=DV)
    S.op("vector", lambda e: e.tensor_copy(
        s_gCp[64:128, :], s_gC[64:128, :].rearrange("p (a two) -> p a two", two=2)[:, :, 1]),
        inc=DV)
    pair_done = (DV, S.val(DV))
    for c in range(8):
        S.op("scalar", lambda e, c=c: nc.scalar.activation(
            s_gpow[:, c * 128:(c + 1) * 128], s_e1[:], AF.Exp,
            scale=s_llgp[:, c: c + 1]), waits=[pair_done], inc=AC)
    for h in range(H):
        hp, lo = divmod(h, 2)
        col = hp * 256 + lo * 128
        S.op("vector", lambda e, h=h: e.scalar_tensor_tensor(
            scr(0, 128), s_idxT[:], s_llg[:, h: h + 1], s_maskT[:],
            op0=ALU.mult, op1=ALU.add), waits=[llg_done, consts_ld], inc=DV)
        S.op("scalar", lambda e, col=col: nc.scalar.activation(
            s_dmat[:, col: col + 128], scr(0, 128), AF.Exp),
            waits=[(DV, S.val(DV))], inc=AC)
        S.op("vector", lambda e: e.engine_nop(), waits=[(AC, S.val(AC))], inc=DV)
    S.op("vector", lambda e: e.tensor_scalar(
        s_gubsc[:].rearrange("p (e j) -> p e j", e=E),
        s_gub[:].rearrange("p (e n) -> p e n", e=E)[:, :, 0:8],
        ALPHA, None, op0=ALU.mult), waits=[consts_ld], inc=DV)
    S.op("vector", lambda e: e.tensor_scalar(
        s_gubu1[:].rearrange("p (e j) -> p e j", e=E),
        s_gub[:].rearrange("p (e n) -> p e n", e=E)[:, :, 8:16],
        1.0, None, op0=ALU.add), waits=[consts_ld], inc=DV)
    S.op("vector", lambda e: e.memset(s_sf[:], 0.0), inc=DV)
    S.op("vector", lambda e: e.memset(s_eps[:], LN_EPS), inc=DV)
    S.op("vector", lambda e: e.memset(s_sbf[:], 0.0), inc=DV)
    prelude_dv = (DV, S.val(DV))
    prelude_ac = (AC, S.val(AC))


    C7 = LIMIT
    SV_CLIP = float(ALPHA * LIMIT)

    dbg_next = [0]

    def dbg_dump(src_ap, waits):
        if dbg is None or dbg_next[0] >= n_dbg:
            return
        i = dbg_next[0]
        dbg_next[0] += 1
        S.op("sync", lambda e: e.dma_start(out=dbg[i], in_=src_ap),
             waits=waits, inc=SD)

    for ch in range(NCH):
        b, g = divmod(ch, L // G)
        t0 = g * G
        first = (ch == 0)
        seq_start = (g == 0)

        S.op("gpsimd", lambda e, b=b, t0=t0: e.dma_start(
            out=s_xw[:].rearrange("p (kc t) -> p kc t", kc=KC),
            in_=x_t[b, :, t0:t0 + G].rearrange("(kc p) t -> p kc t", p=128)),
            waits=rd("xw") + ([pre_pd] if first else []), inc=PD)
        xw_ld = (PD, S.val(PD))

        if first:
            S.op("gpsimd", lambda e: e.dma_start(
                out=s_big[:].rearrange("p (kc m) -> p kc m", kc=KC),
                in_=wqkv_t.rearrange("(kc p) m -> p kc m", p=128)), inc=PD)
            wq_ld = (PD, S.val(PD))
            S.op("gpsimd", lambda e: e.dma_start(
                out=wqkv_cache.rearrange("(kc p) m -> p kc m", p=128),
                in_=s_big[:].rearrange("p (kc m) -> p kc m", kc=KC)),
                waits=[wq_ld], inc=PD)
            wq_wb = (PD, S.val(PD))
            S.op("gpsimd", lambda e: e.dma_start(
                out=s_r2[:].rearrange("p (kc m) -> p kc m", kc=KC),
                in_=wo_t.rearrange("(kc p) m -> p kc m", p=128)), inc=PD)
            wo_ld = (PD, S.val(PD))
            S.op("gpsimd", lambda e: e.dma_start(
                out=wo_cache.rearrange("(kc p) m -> p kc m", p=128),
                in_=s_r2[:].rearrange("p (kc m) -> p kc m", kc=KC)),
                waits=[wo_ld], inc=PD)
            wo_wb = (PD, S.val(PD))
        else:
            S.op("gpsimd", lambda e: e.dma_start(
                out=s_big[:].rearrange("p (kc m) -> p kc m", kc=KC),
                in_=wqkv_cache.rearrange("(kc p) m -> p kc m", p=128)),
                waits=rd("big") + [wq_wb], inc=PD)
            wq_ld = (PD, S.val(PD))
            S.op("gpsimd", lambda e: e.dma_start(
                out=s_r2[:].rearrange("p (kc m) -> p kc m", kc=KC),
                in_=wo_cache.rearrange("(kc p) m -> p kc m", p=128)),
                waits=rd("r2") + [wo_wb], inc=PD)
            wo_ld = (PD, S.val(PD))

        if seq_start and not first:
            S.op("vector", lambda e: e.memset(s_sf[:], 0.0),
                 waits=rd("state"), inc=DV)
            S.op("vector", lambda e: e.memset(s_sbf[:], 0.0), inc=DV)
        state_zeroed = (DV, S.val(DV)) if seq_start else None

        qT_ready = None
        if plmaj < 2:
            continue
        def emit_qpass():
            nonlocal qT_ready
            for mc in range(KC):
                bank = pA if mc % 2 == 0 else pB
                mms = []
                for kc in range(KC):
                    mms.append(lambda e, kc=kc, mc=mc, bank=bank:
                               nc.tensor.matmul(bank[:], wq_sl(kc, mc * 128, 128),
                                                s_xw[:, kc * G:(kc + 1) * G],
                                                start=(kc == 0), stop=(kc == KC - 1)))
                mm_group(bank, mms, extra_waits=[wq_ld, xw_ld])
                S.op("vector", lambda e, mc=mc, bank=bank: e.tensor_copy(
                    s_qR[:, mc * G:(mc + 1) * G], bank[:]),
                    waits=[bank_full[bname(bank)]] + (rd("qR") if mc == 0 else []),
                    inc=DV, fence=False)
                S_gp = s_gpow[:, mc * 128:(mc + 1) * 128]
                bank_read(bank, "vector",
                          lambda e, mc=mc, bank=bank, S_gp=S_gp: e.tensor_tensor(
                              s_qT[:, mc * G:(mc + 1) * G].rearrange(
                                  "p (tt c) -> p tt c", c=128),
                              bank[:].rearrange("p (tt c) -> p tt c", c=128),
                              S_gp.unsqueeze(1).to_broadcast([128, TT, 128]),
                              op=ALU.mult),
                          extra_waits=[prelude_ac] + rd("qT"), fence=False)
            qT_ready = (DV, S.val(DV))

        if plmaj < 3:
            continue
        kn_ready_m = {}
        v_ready_m = {}
        knT_ready_m = {}

        def emit_kv(tt):
            rot = tt % 2
            for mg in range(4):
                bank = pC if mg % 2 == 0 else pD
                m0 = D + mg * 512
                mms = []
                for kc in range(KC):
                    mms.append(lambda e, kc=kc, m0=m0, tt=tt, bank=bank:
                               nc.tensor.matmul(
                                   bank[:],
                                   s_xw[:, kc * G + tt * 128: kc * G + (tt + 1) * 128],
                                   wq_sl(kc, m0, 512),
                                   start=(kc == 0), stop=(kc == KC - 1)))
                mm_group(bank, mms, extra_waits=[wq_ld, xw_ld])
                if mg < 2:
                    kraw = scr(mg)
                    bank_read(bank, "vector",
                              lambda e, bank=bank, kraw=kraw: e.tensor_copy(
                                  kraw, bank[:]), fence=False)
                    S.op("scalar", lambda e, kraw=kraw: nc.scalar.activation(
                        scr(2), kraw, AF.Square), waits=[(DV, S.val(DV))], inc=AC,
                        fence=False)
                    S.op("vector", lambda e, kraw=kraw, mg=mg: e.tensor_reduce(
                        k16(0)[:, mg * 8:(mg + 1) * 8],
                        kraw.rearrange("p (h d) -> p h d", d=DK),
                        axis=AX.X, op=ALU.add), inc=DV)
                    S.op("vector", lambda e, mg=mg: e.tensor_reduce(
                        k16(1)[:, mg * 8:(mg + 1) * 8],
                        scr(2).rearrange("p (h d) -> p h d", d=DK),
                        axis=AX.X, op=ALU.add), waits=[(AC, S.val(AC))], inc=DV,
                        fence=False)
                else:
                    if mg == 2:
                        emit_stats(tt, 0)
                    v0 = (mg - 2) * 512
                    bank_read(bank, "vector",
                              lambda e, bank=bank, v0=v0, rot=rot: e.tensor_copy(
                                  s_v[:, rot * D + v0: rot * D + v0 + 512],
                                  bank[:]), extra_waits=rd(("v", rot)),
                              fence=False)
            emit_stats(tt, 1)
            v_ready_m[tt] = (DV, S.val(DV))

        def emit_stats(tt, mg):
            rot = tt % 2
            hs = slice(mg * 8, (mg + 1) * 8)
            S.op("vector", lambda e, hs=hs: e.tensor_scalar(
                k16(2)[:, hs], k16(0)[:, hs], 1.0 / DK, None, op0=ALU.mult),
                inc=DV)
            S.op("vector", lambda e, hs=hs: e.tensor_scalar(
                k16(3)[:, hs], k16(1)[:, hs], 1.0 / DK, None, op0=ALU.mult),
                inc=DV)
            S.op("vector", lambda e, hs=hs: e.tensor_tensor(
                k16(4)[:, hs], k16(2)[:, hs], k16(2)[:, hs], op=ALU.mult),
                inc=DV)
            S.op("vector", lambda e, hs=hs: e.tensor_tensor(
                k16(3)[:, hs], k16(3)[:, hs], k16(4)[:, hs], op=ALU.subtract),
                inc=DV)
            S.op("scalar", lambda e, hs=hs: nc.scalar.activation(
                k16(3)[:, hs], k16(3)[:, hs], AF.Ln, bias=s_eps[:]),
                waits=[(DV, S.val(DV)), prelude_dv], inc=AC)
            S.op("scalar", lambda e, hs=hs: nc.scalar.activation(
                k16(3)[:, hs], k16(3)[:, hs], AF.Exp, scale=-0.5), inc=AC)
            S.op("vector", lambda e, hs=hs: e.tensor_tensor(
                k16(4)[:, hs], k16(2)[:, hs], k16(3)[:, hs], op=ALU.mult),
                waits=[(AC, S.val(AC))], inc=DV)
            S.op("vector", lambda e, hs=hs: e.tensor_scalar(
                k16(4)[:, hs], k16(4)[:, hs], -1.0, None, op0=ALU.mult),
                inc=DV)
            kn_dep = (DV, S.val(DV))
            kraw = scr(mg)
            for hh in range(8):
                h = mg * 8 + hh
                S.op("scalar", lambda e, kraw=kraw, hh=hh, h=h, rot=rot:
                     nc.scalar.activation(
                         s_kn[:, rot * D + h * DK: rot * D + (h + 1) * DK],
                         kraw[:, hh * DK:(hh + 1) * DK], AF.Identity,
                         scale=k16(3)[:, h: h + 1], bias=k16(4)[:, h: h + 1]),
                     waits=[kn_dep] + (rd(("kn", rot)) if hh == 0 and mg == 0
                                       else []),
                     inc=AC, fence=False)
            if mg == 1:
                kn_ready_m[tt] = (AC, S.val(AC))

        def emit_transp(tt):
            rot = tt % 2
            mms = []
            for hp in range(8):
                mms.append(lambda e, hp=hp, rot=rot:
                           nc.tensor.transpose(
                               pG[:, hp * 128:(hp + 1) * 128],
                               s_kn[:, rot * D + hp * 128: rot * D + (hp + 1) * 128],
                               s_ident[:]))
            mm_group(pG, mms, extra_waits=[kn_ready_m[tt], consts_ld])
            bank_read(pG, "vector",
                      lambda e, rot=rot: e.tensor_copy(
                          s_knT[:, rot * KC * 128:(rot + 1) * KC * 128], pG[:]),
                      extra_waits=rd(("knT", rot)), fence=False)
            knT_ready_m[tt] = (DV, S.val(DV))

        def emit_retnet(tt):
            rot = tt % 2
            kn_ready = kn_ready_m[tt]
            knT_ready = knT_ready_m[tt]
            v_ready = v_ready_m[tt]
            if plmaj < 4:
                return
            for h in range(H):
                S.op("scalar", lambda e, h=h, rot=rot: nc.scalar.activation(
                    s_kp[:, h * DK:(h + 1) * DK],
                    s_kn[:, rot * D + h * DK: rot * D + (h + 1) * DK],
                    AF.Copy, scale=s_kpow[:, h: h + 1]),
                    waits=[kn_ready] + (rd("kp") if h == 0 else []), inc=AC,
                    fence=False)
            kp_ready = (AC, S.val(AC))
            if phase_limit < 42:
                return
            par = (g * SUB + tt) % 2
            npar = 1 - par
            sdec_marks = {}

            def emit_st(hp):
                bkA, bkB = (pE, pF) if hp % 2 == 0 else (pA, pB)
                h0, h1 = 2 * hp, 2 * hp + 1
                rot2 = hp % 2
                for li, h, bank in ((0, h0, bkA), (1, h1, bkB)):
                    base = 64 * li
                    mm_group(bank, [lambda e, base=base, hp=hp, rot=rot, tt=tt, bank=bank:
                                    nc.tensor.matmul(
                                        bank[0:128, 0:128],
                                        s_knT[base:base + 64,
                                              rot * KC * 128 + hp * 128:
                                              rot * KC * 128 + (hp + 1) * 128],
                                        s_qR[base:base + 64,
                                             hp * G + tt * 128: hp * G + (tt + 1) * 128],
                                        start=True, stop=True)],
                             extra_waits=[knT_ready, qT_ready])
                for li, bank in ((0, bkA), (1, bkB)):
                    bank_read(bank, "vector",
                              lambda e, bank=bank, hp=hp, li=li, rot2=rot2:
                              e.tensor_tensor(
                                  s_sdec[:, rot2 * 256 + li * 128:
                                         rot2 * 256 + (li + 1) * 128],
                                  bank[:, 0:128],
                                  s_dmat[:, hp * 256 + li * 128:
                                         hp * 256 + (li + 1) * 128],
                                  op=ALU.mult),
                              extra_waits=[prelude_ac], fence=False)
                sdec_marks[hp] = (DV, S.val(DV))

            def emit_rest(hp):
                bkA, bkB = (pE, pF) if hp % 2 == 0 else (pA, pB)
                h0, h1 = 2 * hp, 2 * hp + 1
                rot2 = hp % 2
                sdec_done = sdec_marks[hp]
                if phase_limit < 43:
                    return
                mms = []
                for li, h in ((0, h0), (1, h1)):
                    base = 64 * li
                    mms.append(lambda e, li=li, h=h, base=base, rot=rot, rot2=rot2, bank=bkA:
                               nc.tensor.matmul(
                                   bank[base:base + 64, 256:384],
                                   s_v[:, rot * D + h * DK: rot * D + (h + 1) * DK],
                                   s_sdec[:, rot2 * 256 + li * 128:
                                          rot2 * 256 + (li + 1) * 128],
                                   start=True, stop=False))
                    mms.append(lambda e, li=li, base=base, hp=hp, par=par, bank=bkA, tt=tt:
                               nc.tensor.matmul(
                                   bank[base:base + 64, 256:384],
                                   sbf_sl(par, hp)[base:base + 64, :],
                                   s_qT[base:base + 64,
                                        hp * G + tt * 128: hp * G + (tt + 1) * 128],
                                   start=False, stop=True))
                    if phase_limit >= 44:
                        mms.append(lambda e, li=li, h=h, base=base, rot=rot, bank=bkA:
                                   nc.tensor.matmul(
                                       bank[base:base + 64, 384:448],
                                       s_kp[:, h * DK:(h + 1) * DK],
                                       s_v[:, rot * D + h * DK: rot * D + (h + 1) * DK],
                                       start=True, stop=True))
                extra = [sdec_done, v_ready, kp_ready]
                if state_zeroed is not None:
                    extra.append(state_zeroed)
                mm_group(bkA, mms, extra_waits=extra)
                bank_read(bkA, "vector",
                          lambda e, bank=bkA, hp=hp, tt=tt: e.tensor_copy(
                              s_yT[:, hp * G + tt * 128: hp * G + (tt + 1) * 128],
                              bank[:, 256:384]),
                          extra_waits=rd("yT") if (hp == 0 and tt == 0) else [],
                          fence=False)
                if phase_limit < 44:
                    return
                bank_read(bkA, "vector",
                          lambda e, bank=bkA, hp=hp, par=par, npar=npar:
                          e.scalar_tensor_tensor(
                              sf_sl(npar, hp), sf_sl(par, hp),
                              s_gCp[:, hp: hp + 1], bank[:, 384:448],
                              op0=ALU.mult, op1=ALU.add),
                          extra_waits=[pair_done], fence=False)
                S.op("scalar", lambda e, hp=hp, npar=npar: nc.scalar.activation(
                    sbf_sl(npar, hp), sf_sl(npar, hp), AF.Copy),
                    waits=[(DV, S.val(DV))], inc=AC, fence=False)

            emit_st(0)
            for hp in range(8):
                if hp + 1 < 8:
                    emit_st(hp + 1)
                emit_rest(hp)
            note_read(("kn", rot), AC)
            note_read(("v", rot), PE)
            note_read(("knT", rot), PE)
            note_read("kp", PE)

        emit_kv(0)
        emit_qpass()
        for tt in range(TT):
            emit_transp(tt)
            emit_retnet(tt)
            if tt + 1 < TT:
                emit_kv(tt + 1)
        note_read("state", AC)
        note_read("big", PE)
        yT_ready = (DV, S.val(DV))
        state_bf_ready = (AC, S.val(AC))

        if plmaj < 5:
            continue
        if bname(pC) in bank_free:
            S.op("tensor", lambda e: e.nop(), waits=[bank_free[bname(pC)],
                                                     bank_free[bname(pD)]])
        for mc in range(KC):
            bank = pA if mc % 2 == 0 else pB
            mms = []
            for kc in range(KC):
                mms.append(lambda e, kc=kc, mc=mc, bank=bank:
                           nc.tensor.matmul(bank[:], wo_sl(kc, mc * 128, 128),
                                            s_yT[:, kc * G:(kc + 1) * G],
                                            start=(kc == 0), stop=(kc == KC - 1)))
            mm_group(bank, mms, extra_waits=[wo_ld, yT_ready])
            bank_read(bank, "vector",
                      lambda e, mc=mc, bank=bank: e.tensor_tensor(
                          s_xres[:, mc * G:(mc + 1) * G], bank[:],
                          s_xw[:, mc * G:(mc + 1) * G], op=ALU.add),
                      extra_waits=rd("xres") if mc == 0 else [], fence=False)
            xres_mc = (DV, S.val(DV))
            S.op("scalar", lambda e, mc=mc: nc.scalar.activation(
                s_act[:, mc * G:(mc + 1) * G], s_xres[:, mc * G:(mc + 1) * G],
                AF.Square),
                waits=[xres_mc] + (rd("act") if mc == 0 else []), inc=AC,
                fence=False)
            sq_mc = (AC, S.val(AC))
            S.op("tensor", lambda e, mc=mc: nc.tensor.matmul(
                pC[0:1, 0:G], s_onescol[:], s_xres[:, mc * G:(mc + 1) * G],
                start=(mc == 0), stop=(mc == KC - 1)),
                waits=[xres_mc] + ([consts_ld] if mc == 0 else []),
                inc=PE if mc == KC - 1 else None)
            S.op("tensor", lambda e, mc=mc: nc.tensor.matmul(
                pD[0:1, 0:G], s_onescol[:], s_act[:, mc * G:(mc + 1) * G],
                start=(mc == 0), stop=(mc == KC - 1)),
                waits=[sq_mc], inc=PE if mc == KC - 1 else None)
        bank_full[bname(pC)] = (PE, S.val(PE))
        bank_full[bname(pD)] = (PE, S.val(PE))
        note_read("xw", DV)
        note_read("r2", PE)
        xres_ready = (DV, S.val(DV))
        note_read("act", PE)
        bank_read(pC, "vector", lambda e: e.tensor_scalar(
            lnr(0), pC[0:1, 0:G], 1.0 / D, None, op0=ALU.mult),
            fence=False)
        bank_read(pD, "vector", lambda e: e.tensor_scalar(
            lnr(1), pD[0:1, 0:G], 1.0 / D, None, op0=ALU.mult),
            fence=False)
        S.op("vector", lambda e: e.tensor_tensor(
            lnr(2), lnr(0), lnr(0), op=ALU.mult), inc=DV)
        S.op("vector", lambda e: e.tensor_tensor(
            lnr(1), lnr(1), lnr(2), op=ALU.subtract), inc=DV)
        S.op("scalar", lambda e: nc.scalar.activation(
            lnr(1), lnr(1), AF.Ln, bias=s_eps[0:1, :]),
            waits=[(DV, S.val(DV))], inc=AC)
        S.op("scalar", lambda e: nc.scalar.activation(
            lnr(1), lnr(1), AF.Exp, scale=-0.5), inc=AC)
        S.op("vector", lambda e: e.tensor_copy(lnb(0), lnr(1)),
             waits=[(AC, S.val(AC))], inc=DV)
        S.op("vector", lambda e: e.scalar_tensor_tensor(
            lnr(2), lnr(0), -1.0, lnr(1), op0=ALU.mult, op1=ALU.mult), inc=DV)
        S.op("vector", lambda e: e.tensor_copy(lnb(1), lnr(2)), inc=DV)
        ab_ready = (DV, S.val(DV))
        mm_group(pC, [lambda e: nc.tensor.matmul(
            pC[:, 0:G], s_ones1[:], lnb(0), start=True, stop=True)],
            extra_waits=[ab_ready])
        mm_group(pD, [lambda e: nc.tensor.matmul(
            pD[:, 0:G], s_ones1[:], lnb(1), start=True, stop=True)],
            extra_waits=[ab_ready])
        x1_kc_marks = []
        for kc in range(KC):
            S.op("vector", lambda e, kc=kc: e.tensor_tensor(
                scr(kc % 2), s_xres[:, kc * G:(kc + 1) * G], pC[:, 0:G],
                op=ALU.mult),
                waits=([bank_full[bname(pC)], bank_full[bname(pD)]]
                       if kc == 0 else ()), inc=DV, fence=False)
            if kc >= 1:
                S.op("vector", lambda e, kc=kc: e.tensor_tensor(
                    s_x1[:, (kc - 1) * G: kc * G], scr((kc - 1) % 2), pD[:, 0:G],
                    op=ALU.add),
                    waits=rd("x1") if kc == 1 else (), inc=DV, fence=False)
                x1_kc_marks.append((DV, S.val(DV)))
        S.op("vector", lambda e: e.tensor_tensor(
            s_x1[:, (KC - 1) * G: KC * G], scr((KC - 1) % 2), pD[:, 0:G],
            op=ALU.add), inc=DV)
        x1_kc_marks.append((DV, S.val(DV)))
        x1_ready = (DV, S.val(DV))
        bank_free[bname(pC)] = (DV, S.val(DV))
        bank_free[bname(pD)] = (DV, S.val(DV))


        if plmaj < 6:
            continue
        for tt in range(TT):
            mms = []
            for kc in range(KC):
                mms.append(lambda e, kc=kc, tt=tt: nc.tensor.matmul(
                    pE[:, 0:16],
                    s_x1[:, kc * G + tt * 128: kc * G + (tt + 1) * 128],
                    s_rt[:, kc * E:(kc + 1) * E],
                    start=(kc == 0), stop=(kc == KC - 1)))
            mm_group(pE, mms, extra_waits=[x1_ready, pre_pd])
            bank_read(pE, "vector", lambda e: e.tensor_tensor(
                s_lg[:, 0:16], pE[:, 0:16], s_brep[:], op=ALU.add),
                extra_waits=[consts_ld], fence=False)
            S.op("vector", lambda e: e.tensor_reduce(
                sm(0), s_lg[:, 0:16], axis=AX.X, op=ALU.max, negate=True), inc=DV)
            S.op("scalar", lambda e: nc.scalar.activation(
                s_lg[:, 16:32], s_lg[:, 0:16], AF.Exp, bias=sm(0),
                accum_out=sm(1)), waits=[(DV, S.val(DV))], inc=AC)
            S.op("vector", lambda e: e.reciprocal(sm(2), sm(1)),
                 waits=[(AC, S.val(AC))], inc=DV, fence=False)
            S.op("vector", lambda e: e.tensor_scalar(
                s_lg[:, 32:48], s_lg[:, 16:32], sm(2), None, op0=ALU.mult), inc=DV)
            S.op("vector", lambda e: e.max(out=s_top8[:], in_=s_lg[:, 32:48]), inc=DV)
            S.op("vector", lambda e, tt=tt: e.scalar_tensor_tensor(
                s_w[:, tt * E:(tt + 1) * E], s_lg[:, 32:48], s_top8[:, 3:4],
                s_lg[:, 32:48], op0=ALU.is_ge, op1=ALU.mult),
                waits=rd("w") if tt == 0 else [], inc=DV)
            S.op("vector", lambda e, tt=tt: e.tensor_copy(
                s_wbf[:], s_w[:, tt * E:(tt + 1) * E]), inc=DV)
            w_ready = (DV, S.val(DV))
            mm_group(pG, [lambda e: nc.tensor.transpose(
                pG[0:16, 0:128], s_wbf[:, 0:16], s_ident[:])],
                extra_waits=[w_ready])
            bank_read(pG, "vector", lambda e, tt=tt: e.tensor_copy(
                s_wT[0:16, tt * 128:(tt + 1) * 128], pG[0:16, 0:128]),
                extra_waits=rd("wT") if tt == 0 else [], fence=False)
            wT_ready = (DV, S.val(DV))
            for hh in range(2):
                bank = pE if hh == 0 else pF
                mm_group(bank, [lambda e, tt=tt, hh=hh, bank=bank: nc.tensor.matmul(
                    bank[:, 0:512], s_wT[0:16, tt * 128:(tt + 1) * 128],
                    s_db[0:16, hh * 512:(hh + 1) * 512], start=True, stop=True)],
                    extra_waits=[wT_ready, pre_pd])
                bank_read(bank, "vector", lambda e, tt=tt, hh=hh, bank=bank:
                          e.tensor_copy(
                              s_acc[:, tt * D + hh * 512: tt * D + (hh + 1) * 512],
                              bank[:, 0:512]),
                          extra_waits=rd("acc") if (tt == 0 and hh == 0) else [],
                          fence=False)
        acc_init = (DV, S.val(DV))
        w_all = (DV, S.val(DV))

        for tt in range(TT):
            mms = []
            for kc in range(KC):
                mms.append(lambda e, kc=kc, tt=tt: nc.tensor.transpose(
                    pG[:, kc * 128:(kc + 1) * 128],
                    s_x1[:, kc * G + tt * 128: kc * G + (tt + 1) * 128],
                    s_ident[:]))
            mm_group(pG, mms, extra_waits=[x1_kc_marks[KC - 1], consts_ld])
            bank_read(pG, "vector",
                      lambda e, tt=tt: e.tensor_copy(x1tok_sl(tt), pG[:]),
                      fence=False)
        x1tok_ready = (DV, S.val(DV))


        if plmaj < 7:
            continue
        scr_reader = {}
        for ex in range(E):
            if first:
                S.op("gpsimd", lambda e, ex=ex: e.dma_start(
                    out=s_big[:, 0:KC * GU].rearrange("p (kc n) -> p kc n", kc=KC),
                    in_=gu_proj[ex].rearrange("(kc p) n -> p kc n", p=128)),
                    waits=rd("big") + [wq_wb], inc=PD)
                gu_ld = (PD, S.val(PD))
                S.op("gpsimd", lambda e, ex=ex: e.dma_start(
                    out=gu_cache[ex].rearrange("(kc p) n -> p kc n", p=128),
                    in_=s_big[:, 0:KC * GU].rearrange("p (kc n) -> p kc n", kc=KC)),
                    waits=[gu_ld], inc=PD)
                S.op("gpsimd", lambda e, ex=ex: e.dma_start(
                    out=s_r2[:, 0:KC * D].rearrange("p (kc n) -> p kc n", kc=KC),
                    in_=down_proj[ex].rearrange("(kc p) n -> p kc n", p=128)),
                    waits=rd("r2") + [wo_wb], inc=PD)
                dn_ld = (PD, S.val(PD))
                S.op("gpsimd", lambda e, ex=ex: e.dma_start(
                    out=down_cache[ex].rearrange("(kc p) n -> p kc n", p=128),
                    in_=s_r2[:, 0:KC * D].rearrange("p (kc n) -> p kc n", kc=KC)),
                    waits=[dn_ld], inc=PD)
                if ex == E - 1:
                    gu_wb = (PD, S.val(PD))
            else:
                S.op("gpsimd", lambda e, ex=ex: e.dma_start(
                    out=s_big[:, 0:KC * GU].rearrange("p (kc n) -> p kc n", kc=KC),
                    in_=gu_cache[ex].rearrange("(kc p) n -> p kc n", p=128)),
                    waits=rd("big") + [gu_wb], inc=PD)
                gu_ld = (PD, S.val(PD))
                S.op("gpsimd", lambda e, ex=ex: e.dma_start(
                    out=s_r2[:, 0:KC * D].rearrange("p (kc n) -> p kc n", kc=KC),
                    in_=down_cache[ex].rearrange("(kc p) n -> p kc n", p=128)),
                    waits=rd("r2") + [gu_wb], inc=PD)
                dn_ld = (PD, S.val(PD))

            for j in range(KC):
                gbank = pA if j % 2 == 0 else pB
                ubank = pC if j % 2 == 0 else pD
                mms = []
                for kc in range(KC):
                    mms.append(lambda e, kc=kc, j=j, gbank=gbank: nc.tensor.matmul(
                        gbank[:], gu_sl(kc, j * 128, 128), s_x1[:, kc * G:(kc + 1) * G],
                        start=(kc == 0), stop=(kc == KC - 1)))
                mm_group(gbank, mms, extra_waits=[gu_ld, x1_ready])
                mms = []
                for kc in range(KC):
                    mms.append(lambda e, kc=kc, j=j, ubank=ubank: nc.tensor.matmul(
                        ubank[:], gu_sl(kc, NI + j * 128, 128),
                        s_x1[:, kc * G:(kc + 1) * G],
                        start=(kc == 0), stop=(kc == KC - 1)))
                mm_group(ubank, mms, extra_waits=[gu_ld, x1_ready])
                svs = scr(0 + (j % 2))
                bank_read(gbank, "scalar",
                          lambda e, gbank=gbank, ex=ex, j=j, svs=svs:
                          nc.scalar.activation(
                              svs, gbank[:], AF.Silu, scale=ALPHA,
                              bias=s_gubsc[:, ex * 8 + j: ex * 8 + j + 1]),
                          extra_waits=[prelude_dv] +
                          scr_reader.get(("sv", j % 2), []), fence=False)
                sv_done = (AC, S.val(AC))
                S.op("vector", lambda e, svs=svs: e.tensor_scalar(
                    svs, svs, SV_CLIP, 1.0 / ALPHA, op0=ALU.min, op1=ALU.mult),
                    waits=[sv_done], inc=DV, fence=False)
                u2 = scr(2 + (j % 2))
                bank_read(ubank, "vector",
                          lambda e, ubank=ubank, ex=ex, j=j, u2=u2: e.tensor_scalar(
                              u2, ubank[:],
                              s_gubu1[:, ex * 8 + j: ex * 8 + j + 1],
                              LIMIT + 1.0, op0=ALU.add, op1=ALU.min),
                          extra_waits=[prelude_dv] +
                          scr_reader.get(("u", j % 2), []), fence=False)
                S.op("vector", lambda e, j=j, u2=u2, svs=svs: e.scalar_tensor_tensor(
                    s_act[:, j * G:(j + 1) * G], u2, -(LIMIT - 1.0), svs,
                    op0=ALU.max, op1=ALU.mult),
                    waits=(rd("act") if (j == 0 and ex == 0) else []), inc=DV)
                scr_reader[("sv", j % 2)] = [(DV, S.val(DV))]
                scr_reader[("u", j % 2)] = [(DV, S.val(DV))]
            act_ready = (DV, S.val(DV))
            note_read("big", PE)

            for tt in range(TT):
                for hh in range(2):
                    bank = pE if (tt * 2 + hh) % 2 == 0 else pF
                    mms = []
                    for kc in range(KC):
                        mms.append(lambda e, kc=kc, tt=tt, hh=hh, bank=bank:
                                   nc.tensor.matmul(
                                       bank[:],
                                       s_act[:, kc * G + tt * 128: kc * G + (tt + 1) * 128],
                                       dn_sl(kc, hh * 512, 512),
                                       start=(kc == 0), stop=(kc == KC - 1)))
                    mm_group(bank, mms, extra_waits=[act_ready, dn_ld, acc_init])
                    bank_read(bank, "vector",
                              lambda e, tt=tt, hh=hh, ex=ex, bank=bank:
                              e.scalar_tensor_tensor(
                                  s_acc[:, tt * D + hh * 512: tt * D + (hh + 1) * 512],
                                  bank[:], s_w[:, tt * E + ex: tt * E + ex + 1],
                                  s_acc[:, tt * D + hh * 512: tt * D + (hh + 1) * 512],
                                  op0=ALU.mult, op1=ALU.add),
                              extra_waits=[w_all], fence=False)
            note_read("r2", PE)
        acc_ready = (DV, S.val(DV))
        note_read("w", DV)
        note_read("wT", PE)

        if plmaj < 8:
            continue
        for tt in range(TT):
            rot = tt % 2
            S.op("vector", lambda e, tt=tt, rot=rot: e.tensor_tensor(
                s_r[:, rot * D:(rot + 1) * D], s_acc[:, tt * D:(tt + 1) * D],
                x1tok_sl(tt), op=ALU.add),
                waits=[acc_ready, x1tok_ready] + rd(("r", rot)), inc=DV,
                fence=False)
            S.op("vector", lambda e, rot=rot: e.tensor_reduce(
                sm(3), s_r[:, rot * D:(rot + 1) * D], axis=AX.X, op=ALU.add,
                negate=True), inc=DV)
            S.op("vector", lambda e: e.tensor_scalar(
                sm(3), sm(3), 1.0 / D, None, op0=ALU.mult), inc=DV)
            r_done = (DV, S.val(DV))
            S.op("scalar", lambda e, rot=rot: nc.scalar.activation(
                s_r[:, (1 - rot) * D:(2 - rot) * D], s_r[:, rot * D:(rot + 1) * D],
                AF.Square, bias=sm(3), accum_out=sm(4)), waits=[r_done], inc=AC,
                fence=False)
            S.op("scalar", lambda e: nc.scalar.activation(
                sm(5), sm(4), AF.Ln, scale=1.0 / D, bias=s_eps[:]), inc=AC)
            S.op("scalar", lambda e: nc.scalar.activation(
                sm(5), sm(5), AF.Exp, scale=-0.5), inc=AC)
            S.op("vector", lambda e, tt=tt, rot=rot: e.tensor_scalar(
                s_stage[:, rot * D:(rot + 1) * D], s_r[:, rot * D:(rot + 1) * D],
                sm(3), sm(5), op0=ALU.add, op1=ALU.mult),
                waits=[(AC, S.val(AC))] + rd(("stage", rot)), inc=DV)
            note_read(("r", rot), DV)
            S.op("sync", lambda e, b=b, t0=t0, tt=tt, rot=rot: e.dma_start(
                out=out[b, t0 + tt * 128: t0 + (tt + 1) * 128, :],
                in_=s_stage[:, rot * D:(rot + 1) * D]),
                waits=[(DV, S.val(DV))], inc=SD)
            note_read(("stage", rot), SD)
        note_read("acc", DV)
        note_read("xres", DV)
        note_read("x1", PE)
        note_read("qT", PE)
        note_read("qR", PE)
        note_read("yT", PE)

    if dump:
        dump_list = [s_xw, s_qR, s_qT, s_kn, s_knT, s_v, s_yT, s_xres, s_x1,
                     s_w, s_acc, s_sf, s_sbf, s_sdec, s_kp, s_stage, s_r,
                     s_llg, s_gpow, s_dmat, s_kpow, s_gCp, s_lgbc, s_scr,
                     s_act]
        allw = [(PE, S.val(PE)), (DV, S.val(DV)), (AC, S.val(AC)),
                (SD, S.val(SD)), (PD, S.val(PD))]
        for t in dump_list:
            nm = t.name if isinstance(t.name, str) else t.name()
            dt = nc.dram_tensor("dbg_" + nm, list(t.shape), t.dtype,
                                kind="ExternalOutput").ap()
            S.op("sync", lambda e, t=t, dt=dt: e.dma_start(out=dt, in_=t[:]),
                 waits=allw, inc=SD)

    S.op("sync", lambda e: e.nop(), waits=[(SD, S.val(SD))])

    with nc.Block() as block:
        S.emit_all(block)
    return nc


def _consts():
    i_idx = np.arange(128, dtype=np.float32)
    idxT = np.maximum(i_idx[None, :] - i_idx[:, None], 0.0).astype(np.float32)
    maskT = np.where(i_idx[None, :] >= i_idx[:, None], 0.0, NEG_BIG).astype(np.float32)
    e1 = np.broadcast_to(i_idx[None, :] + 1.0, (128, 128)).astype(np.float32).copy()
    c2 = (127.0 - i_idx)[:, None].astype(np.float32).copy()
    ident = np.eye(128, dtype=np.float32).astype(bfdt)
    ones1 = np.ones((1, 128), dtype=np.float32).astype(bfdt)
    onescol = np.ones((128, 1), dtype=np.float32).astype(bfdt)
    return dict(c_idxT=idxT, c_maskT=maskT, c_e1=e1, c_c2=c2, c_ident=ident,
                c_ones1=ones1, c_onescol=onescol)


def kernel(L0_attn_x, L0_attn_W_qkv, L0_attn_W_o, L0_attn_log_gamma,
           L0_ffn_router_weight, L0_ffn_router_bias, L0_ffn_gate_up_proj,
           L0_ffn_gate_up_bias, L0_ffn_down_proj, L0_ffn_down_bias):
    B, L, _ = L0_attn_x.shape
    ncores = 8
    nseq = B // ncores
    nc = build_nc(NSEQ=nseq, L=L, G=512)

    xt = np.ascontiguousarray(np.transpose(np.asarray(L0_attn_x), (0, 2, 1)))
    wqkv_t = np.ascontiguousarray(np.asarray(L0_attn_W_qkv).T)
    wo_t = np.ascontiguousarray(np.asarray(L0_attn_W_o).T)
    router_t = np.ascontiguousarray(np.asarray(L0_ffn_router_weight).T)
    gub_t = np.ascontiguousarray(
        np.asarray(L0_ffn_gate_up_bias).reshape(16, 16, 128).transpose(2, 0, 1)
        .reshape(128, 256))
    consts = _consts()

    base = dict(wqkv_t=wqkv_t, wo_t=wo_t, router_t=router_t,
                log_gamma=np.asarray(L0_attn_log_gamma),
                r_bias=np.asarray(L0_ffn_router_bias), gub_t=gub_t,
                gu_proj=np.asarray(L0_ffn_gate_up_proj),
                down_proj=np.asarray(L0_ffn_down_proj),
                down_bias=np.asarray(L0_ffn_down_bias), **consts)
    in_maps = []
    for c in range(ncores):
        m = dict(base)
        m["x_t"] = np.ascontiguousarray(xt[c * nseq:(c + 1) * nseq])
        in_maps.append(m)

    res = run_bass_kernel_spmd(nc, in_maps, core_ids=list(range(ncores)))
    return np.concatenate([r["out"] for r in res.results], axis=0)
